# version 11
# baseline (speedup 1.0000x reference)
import numpy as np
import ml_dtypes
import concourse.bass as bass
import concourse.mybir as mybir
from concourse.bass_utils import run_bass_kernel_spmd

F32 = mybir.dt.float32
BF16 = mybir.dt.bfloat16
I32 = mybir.dt.int32
AF = mybir.ActivationFunctionType
ALU = mybir.AluOpType

NCORES = 8
R = 1024
NTS = 8
DM = 4096
NH = 16
SCALE = 1.0 / 16.0
SOFTMAX_C = 10.0
LN_EPS = 1e-5


def _build_tables():
    g = np.array([1.0, 1.0, 1.0], dtype=np.float32)
    dim = len(g)
    n = 2 ** dim
    masks = sorted(range(n), key=lambda m: (bin(m).count('1'), m))
    inv = {m: i for i, m in enumerate(masks)}
    C = np.zeros((n, n, n), dtype=np.float32)
    for i, a in enumerate(masks):
        for j, b in enumerate(masks):
            sw, at = (0, a >> 1)
            while at:
                sw += bin(at & b).count('1')
                at >>= 1
            s = -1.0 if sw & 1 else 1.0
            for t in range(dim):
                if a >> t & 1 and b >> t & 1:
                    s *= float(g[t])
            C[i, j, inv[a ^ b]] = s
    grades = np.array([bin(m).count('1') for m in masks])
    rev = np.where(grades * (grades - 1) // 2 % 2 == 1, -1.0, 1.0).astype(np.float32)
    met = np.array([float(np.prod([g[t] for t in range(dim) if m >> t & 1] + [1.0]))
                    for m in masks], dtype=np.float32)
    return C, rev, met


def _host_consts():
    C, rev, met = _build_tables()
    biv = C[:, :, [4, 5, 6]]
    bdc = np.zeros((4, 128, 128), dtype=np.float32)
    blk = rev[None, :, None] * biv.transpose(2, 0, 1)
    diag = SCALE * np.diag(rev * met)
    for d in range(16):
        sl = slice(d * 8, d * 8 + 8)
        for c in range(3):
            bdc[c][sl, sl] = blk[c]
        bdc[3][sl, sl] = diag
    return bdc.reshape(4, 128 * 128), np.eye(128).astype(ml_dtypes.bfloat16)


class Sem:
    def __init__(self, nc, name):
        self.h = nc.alloc_semaphore(name)
        self.n = 0


class Em:
    ENGS = ("sync", "gpsimd", "vector", "scalar", "tensor")

    def __init__(self):
        self.q = {e: [] for e in self.ENGS}
        self.last_wait = {e: {} for e in self.ENGS}

    def op(self, eng, fn, inc=None, inc_n=1, waits=()):
        ws = []
        for w_ in waits:
            if w_ is None:
                continue
            (s, v) = w_
            if s is None or v is None or v <= 0:
                continue
            key = id(s)
            if self.last_wait[eng].get(key, -1) >= v:
                continue
            self.last_wait[eng][key] = v
            ws.append((s.h, v))
        iv = None
        if inc is not None:
            inc.n += inc_n
            iv = (inc.h, inc_n)
        self.q[eng].append((ws, fn, iv))

    def wait(self, eng, waits):
        self.op(eng, None, waits=waits)

    def replay(self, eng_name, eng):
        for (ws, fn, iv) in self.q[eng_name]:
            for (sh, v) in ws:
                eng.wait_ge(sh, v)
            if fn is not None:
                ins = fn(eng)
                if iv is not None:
                    ins.then_inc(iv[0], iv[1])
            elif iv is not None:
                eng.sem_inc(iv[0], iv[1])


def build_nc(debug=False, phases=8):
    nc = bass.Bass("TRN2", target_bir_lowering=False)

    x_d = nc.dram_tensor("x", [R, DM], F32, kind="ExternalInput")
    wqT_d = nc.dram_tensor("wqT", [DM, DM], F32, kind="ExternalInput")
    wkT_d = nc.dram_tensor("wkT", [DM, DM], F32, kind="ExternalInput")
    wvT_d = nc.dram_tensor("wvT", [DM, DM], F32, kind="ExternalInput")
    qwT_d = nc.dram_tensor("qwT", [2048, DM], I32, kind="ExternalInput")
    bq_d = nc.dram_tensor("bq_t", [128, 32], F32, kind="ExternalInput")
    bk_d = nc.dram_tensor("bk_t", [128, 32], F32, kind="ExternalInput")
    bv_d = nc.dram_tensor("bv_t", [128, 32], F32, kind="ExternalInput")
    bwT_d = nc.dram_tensor("bwT", [3, 16], F32, kind="ExternalInput")
    fs_d = nc.dram_tensor("fscale", [1, DM], F32, kind="ExternalInput")
    fz_d = nc.dram_tensor("fzero", [1, DM], F32, kind="ExternalInput")
    fb_d = nc.dram_tensor("fbias", [1, DM], F32, kind="ExternalInput")
    eye_d = nc.dram_tensor("eye", [128, 128], BF16, kind="ExternalInput")
    bdc_d = nc.dram_tensor("bdc", [4, 16384], F32, kind="ExternalInput")
    out_d = nc.dram_tensor("out", [R, DM], F32, kind="ExternalOutput")

    ck_d = nc.dram_tensor("ck", [DM, R], BF16, kind="Internal")
    cv_d = nc.dram_tensor("cv", [R, DM], BF16, kind="Internal")
    gk_d = nc.dram_tensor("gk", [4 * DM, R], BF16, kind="Internal")
    gv_d = nc.dram_tensor("gv", [4 * R, DM], BF16, kind="Internal")
    ao_d = nc.dram_tensor("ao", [R, DM], BF16, kind="Internal")
    bdg_d = nc.dram_tensor("bdg", [16, 16384], BF16, kind="Internal")
    if debug:
        dbg_a = nc.dram_tensor("dbg_a", [128, 32768], BF16, kind="ExternalOutput")
        dbg_b = nc.dram_tensor("dbg_b", [128, 32768], BF16, kind="ExternalOutput")
        dbg_gk = nc.dram_tensor("dbg_gk", [4 * DM, R], BF16, kind="ExternalOutput")
        dbg_gv = nc.dram_tensor("dbg_gv", [4 * R, DM], BF16, kind="ExternalOutput")
        dbg_ao = nc.dram_tensor("dbg_ao", [R, DM], BF16, kind="ExternalOutput")
        dbg_bd = nc.dram_tensor("dbg_bd", [16, 16384], BF16, kind="ExternalOutput")
        dbg_h2 = nc.dram_tensor("dbg_h2", [128, 32768], BF16, kind="ExternalOutput")
        dbg_x2 = nc.dram_tensor("dbg_x2", [128, 32768], BF16, kind="ExternalOutput")
        dbg_rw = nc.dram_tensor("dbg_rw", [1, 7232], I32, kind="ExternalOutput")
        dbg_sb = nc.dram_tensor("dbg_sb", [128, 4096], F32, kind="ExternalOutput")

    BIGA = nc.alloc_sbuf_tensor("BIGA", [128, 32768], BF16)
    BIGB = nc.alloc_sbuf_tensor("BIGB", [128, 32768], BF16)
    BIGW = nc.alloc_sbuf_tensor("BIGW", [128, 16384], BF16)
    BIGV = nc.alloc_sbuf_tensor("BIGV", [128, 16448], BF16)
    SLOT = nc.alloc_sbuf_tensor("SLOT", [128, 4096], BF16)
    PTS = nc.alloc_sbuf_tensor("PTS", [128, 128], BF16)
    SMALL = nc.alloc_sbuf_tensor("SMALL", [128, 256], F32)
    SMALL2 = nc.alloc_sbuf_tensor("SMALL2", [128, 1024], F32)

    PS = [nc.alloc_psum_tensor("PS%d" % i, [128, 512], F32) for i in range(8)]
    EYE = PTS.ap()

    BIGWf = BIGW.ap().bitcast(F32)
    BIGWi = BIGW.ap().bitcast(I32)
    BIGVf = BIGV.ap().bitcast(F32)
    BIGBf = BIGB.ap().bitcast(F32)
    S2bf = SMALL2.ap().bitcast(BF16)

    h1T = BIGA.ap()
    qT = BIGB.ap()
    x2res = BIGB.ap()
    BDIAG = S2bf
    SM = SMALL.ap()
    BQ, BK, BV = SM[:, 64:96], SM[:, 96:128], SM[:, 128:160]
    BW4 = SM[0:4, 160:176]
    SMbf = SMALL.ap().bitcast(BF16)
    ONES128 = SMbf[:, 511:512]

    ZNEG = BIGV.ap()[0:1, 0:4096]
    BOS = BIGV.ap()[0:1, 4096:8192]
    SROWBF = BIGV.ap()[0:1, 8192:12288]
    S_ROW = BIGV.ap()[0:1, 12288:13312]
    ONES_ROW = BIGV.ap()[0:1, 14336:14464]

    dCst = Sem(nc, "dCst")
    dXT = [Sem(nc, "dXT0"), Sem(nc, "dXT1")]
    dSg = [Sem(nc, "dSg0"), Sem(nc, "dSg1")]
    dKs = [Sem(nc, "dKs0"), Sem(nc, "dKs1")]
    dVs = [Sem(nc, "dVs0"), Sem(nc, "dVs1")]
    dKT = [Sem(nc, "dKT0"), Sem(nc, "dKT1")]
    dVh = [Sem(nc, "dVh0"), Sem(nc, "dVh1")]
    dAo = [Sem(nc, "dAo0"), Sem(nc, "dAo1")]
    dP6 = [Sem(nc, "dP60"), Sem(nc, "dP61")]
    dRow = Sem(nc, "dRow")
    dRowB = Sem(nc, "dRowB")
    dQw = [Sem(nc, "dQw0"), Sem(nc, "dQw1")]
    dOut = [Sem(nc, "dOut0"), Sem(nc, "dOut1")]
    dDbg = Sem(nc, "dDbg")
    CC = Sem(nc, "CC")
    sV = Sem(nc, "sV")
    sS = Sem(nc, "sS")
    sT = Sem(nc, "sT")

    E = Em()
    ev = {}

    nc.const_aps.register(F32, -SOFTMAX_C, SM[:, 56:57])
    nc.const_aps.register(F32, -0.5, SM[:, 57:58])

    E.op("sync", lambda e: e.dma_start(EYE, eye_d.ap()), inc=dCst, inc_n=16)
    E.op("sync", lambda e: e.dma_start(BQ, bq_d.ap()), inc=dCst, inc_n=16)
    E.op("sync", lambda e: e.dma_start(BK, bk_d.ap()), inc=dCst, inc_n=16)
    E.op("sync", lambda e: e.dma_start(BV, bv_d.ap()), inc=dCst, inc_n=16)
    E.op("vector", lambda e: e.memset(SM[0:4, 160:176], 1.0), inc=sV)
    ev["bw4memset"] = sV.n
    E.op("vector", lambda e: e.memset(SM[:, 56:57], -SOFTMAX_C), inc=sV)
    E.op("vector", lambda e: e.memset(SM[:, 57:58], -0.5), inc=sV)
    ev["cmemset"] = sV.n
    E.op("vector", lambda e: e.memset(ONES128, 1.0), inc=sV)
    E.op("sync", lambda e: e.dma_start(SM[0:3, 160:176], bwT_d.ap()),
         inc=dCst, inc_n=16, waits=[(sV, ev["bw4memset"])])
    ev["consts_loaded"] = dCst.n

    for ch in range(32):
        bdct = BIGWf[0:4, (ch % 2) * 512:(ch % 2) * 512 + 512]
        E.op("sync",
             (lambda bdct, ch: lambda e: e.dma_start(
                 bdct, bdc_d.ap()[:, ch * 512:ch * 512 + 512]))(bdct, ch),
             inc=dSg[ch % 2], inc_n=16, waits=[(sT, ev.get(("bdg_mm", ch - 2)))])
        ldv = (dSg[ch % 2], dSg[ch % 2].n)
        E.op("tensor",
             (lambda bdct: lambda e: e.matmul(
                 PS[5][0:16, :], BW4, bdct, start=True, stop=True))(bdct),
             inc=sT, waits=[ldv, (sV, ev.get(("bdg_ev", ch - 1))),
                            (dCst, ev["consts_loaded"])])
        ev[("bdg_mm", ch)] = sT.n
        E.op("vector",
             (lambda ch: lambda e: e.tensor_copy(
                 BIGB.ap()[0:16, ch * 512:ch * 512 + 512], PS[5][0:16, :]))(ch),
             inc=sV, waits=[(sT, ev[("bdg_mm", ch)])])
        ev[("bdg_ev", ch)] = sV.n
    E.op("sync", lambda e: e.dma_start(bdg_d.ap(), BIGB.ap()[0:16, 0:16384]),
         inc=dCst, inc_n=16, waits=[(sV, ev[("bdg_ev", 31)])])
    ev["bdg_written"] = dCst.n

    def emit_ln(src, ts, h1dst, phase):
        lb = 192 * (ts % 2)
        bnst = SM[:, lb + 0:lb + 48]
        bnag = SM[:, lb + 48:lb + 50]
        tv = SM[:, lb + 50:lb + 51]
        tl = SM[:, lb + 51:lb + 52]
        rstd = SM[:, lb + 52:lb + 53]
        mrs = SM[:, lb + 53:lb + 54]
        for i in range(8):
            w = []
            if i == 0:
                w = [(sS, ev.get((phase, "lnuse", ts - 2)))]
                if phase == "p1":
                    w.append(ev[(phase, "ld", ts)])
                else:
                    t_ = ev[("p6", "x2", ts)]
                    w.append((CC, t_[1]) if isinstance(t_, tuple) and t_[0] == "CCg" else t_)
            E.op("vector",
                 (lambda i: lambda e: e.bn_stats(
                     SM[:, lb + 6 * i:lb + 6 * i + 6],
                     src[:, 512 * i:512 * i + 512]))(i),
                 inc=(sV if i == 7 else None), waits=w)
        ev[(phase, "bn", ts)] = sV.n
        E.op("vector", lambda e: e.bn_aggr(bnag, bnst),
             inc=sV, waits=[(sV, ev[(phase, "bn", ts)])])
        E.op("vector", lambda e: e.tensor_scalar(tv, SM[:, lb + 49:lb + 50], 1.0,
                                                 LN_EPS, ALU.mult, ALU.add),
             inc=sV, waits=[(sV, sV.n)])
        vv = sV.n
        E.op("scalar", lambda e: e.activation(tl, tv, AF.Ln),
             inc=sS, waits=[(sV, vv)])
        E.op("scalar", lambda e: e.activation(rstd, tl, AF.Exp, scale=-0.5),
             inc=sS, waits=[(sS, sS.n), (sV, ev["cmemset"])])
        sv_ = sS.n
        E.op("vector", lambda e: e.scalar_tensor_tensor(
            mrs, SM[:, lb + 48:lb + 49], -1.0, rstd, ALU.mult, ALU.mult),
            inc=sV, waits=[(sS, sv_)])
        vv2 = sV.n
        E.op("scalar", lambda e: e.activation(h1dst, src, AF.Identity,
                                              bias=mrs, scale=rstd),
             inc=sS, waits=[(sV, vv2),
                            (sT, ev.get((phase, "trdone", ts - 2)))])
        ev[(phase, "h1", ts)] = sS.n
        ev[(phase, "lnuse", ts)] = sS.n

    for ts in range(NTS):
        par = ts % 2
        xt = BIGVf[:, par * 4096:par * 4096 + 4096]
        E.op("scalar",
             (lambda ts, xt: lambda e: e.dma_start(
                 xt, x_d.ap()[ts * 128:ts * 128 + 128, :]))(ts, xt),
             inc=dXT[par], inc_n=16,
             waits=[(sS, ev.get(("p1", "h1", ts - 2)))])
        ev[("p1", "ld", ts)] = (dXT[par], dXT[par].n)
        h1tile = BIGW.ap()[:, par * 4096:par * 4096 + 4096]
        emit_ln(xt, ts, h1tile, "p1")
        for g in range(4):
            gg = ts * 4 + g
            bank = PS[6 + gg % 2]
            bankbf = bank.ap().bitcast(BF16)
            w = [(sS, ev[("p1", "h1", ts)]),
                 (sV, ev.get(("p1trev", gg - 2)))]
            for k in range(8):
                cc = g * 8 + k
                E.op("tensor",
                     (lambda cc, k, bankbf, h1tile: lambda e: e.transpose(
                         bankbf[:, k * 128:k * 128 + 128],
                         h1tile[:, cc * 128:cc * 128 + 128], EYE))(cc, k, bankbf, h1tile),
                     inc=(sT if k == 7 else None), waits=(w if k == 0 else []))
            ev[("p1tr", gg)] = sT.n
            dst = h1T.rearrange("p (cc t) -> p cc t", t=1024)[
                :, g * 8:g * 8 + 8, ts * 128:ts * 128 + 128]
            E.op("vector",
                 (lambda dst, bankbf: lambda e: e.tensor_copy(
                     dst, bankbf[:, 0:1024].rearrange("p (a t) -> p a t", a=8)))(dst, bankbf),
                 inc=sV, waits=[(sT, ev[("p1tr", gg)])])
            ev[("p1trev", gg)] = sV.n
        ev[("p1", "trdone", ts)] = sT.n
        if ts == 3:
            ev[("h1T_half", 0)] = sV.n
    ev[("h1T_half", 1)] = sV.n
    ev["h1T_done"] = sV.n


    E.op("sync",
         lambda e: e.dma_start(
             BDIAG[:, 0:2048].rearrange("p (h c) -> p h c", h=16),
             bdg_d.ap().rearrange("h (p c) -> p h c", p=128)),
         inc=dCst, inc_n=16, waits=[(dCst, ev["bdg_written"])])
    ev["bdiag_ready"] = dCst.n

    def _finish():
        allsems = [dCst, dRow, dRowB, dDbg] + dAo + dXT + dSg + dKs + dVs + dKT + dVh + dP6 + dQw + dOut
        E.wait("sync", [(s_, s_.n) for s_ in allsems if s_.n > 0])
        E.wait("gpsimd", [(CC, CC.n)] + [(s_, s_.n) for s_ in dQw + dOut if s_.n > 0])
        with nc.Block() as blk:
            blk.sync(lambda e: E.replay("sync", e))
            blk.gpsimd(lambda e: E.replay("gpsimd", e))
            blk.vector(lambda e: E.replay("vector", e))
            blk.scalar(lambda e: E.replay("scalar", e))
            blk.tensor(lambda e: E.replay("tensor", e))
        return nc

    if phases <= 1:
        return _finish()

    PREVW = {"wk": None, "wv": "wk", "wq": "wv"}

    def emit_strip_ld(wname, w_dram, ofG):
        buf = ofG % 2
        strip = BIGW.ap()[:, buf * 8192:buf * 8192 + 8192]
        prev = PREVW[wname]
        for hf in range(2):
            idx = ofG * 2 + hf
            sb = idx % 2
            stage = BIGVf[:, sb * 4096:sb * 4096 + 4096]
            wst = [(sV, ev.get((wname, "cv", idx - 2))),
                   (sS, ev.get((wname, "cvs", idx - 2)))]
            if idx < 2 and prev is not None:
                wst.append((sV, ev.get((prev, "cv", 30 + idx))))
                wst.append((sS, ev.get((prev, "cvs", 30 + idx))))
            if wname == "wk" and idx < 2:
                wst.append((sT, ev[("bdg_mm", 31)]))
                wst.append((sS, ev[("p1", "h1", NTS - 1)]))
            E.op("sync",
                 (lambda stage, idx: lambda e: e.dma_start(
                     stage,
                     w_dram.ap().rearrange("(i p) m -> i p m", p=128)[idx]))(stage, idx),
                 inc=dSg[sb], inc_n=16, waits=wst)
            ev[(wname, "stageld", idx)] = (dSg[sb], dSg[sb].n)

    def emit_strip_cv(wname, ofG):
        buf = ofG % 2
        strip = BIGW.ap()[:, buf * 8192:buf * 8192 + 8192]
        prev = PREVW[wname]
        for hf in range(2):
            idx = ofG * 2 + hf
            sb = idx % 2
            stage = BIGVf[:, sb * 4096:sb * 4096 + 4096]
            sdst = strip.rearrange("p (cc c) -> p cc c", c=256)[:, :, hf * 128:hf * 128 + 128]
            ssrc = stage.rearrange("p (cc c) -> p cc c", cc=32)
            cw = [ev[(wname, "stageld", idx)],
                  (sT, ev.get((wname, "stripuse", ofG - 2)))]
            if ofG < 2 and prev is not None:
                cw.append((sT, ev.get((prev, "stripuse", 14 + ofG))))
            if ofG < 2 and wname == "wk":
                cw.append((sT, ev.get(("p1", "trdone", NTS - 1))))
            if hf == 0:
                E.op("vector",
                     (lambda sdst, ssrc: lambda e: e.tensor_copy(sdst, ssrc))(sdst, ssrc),
                     inc=sV, waits=cw)
                ev[(wname, "cv", idx)] = sV.n
                ev[(wname, "cvs", idx)] = None
            else:
                E.op("scalar",
                     (lambda sdst, ssrc: lambda e: e.copy(sdst, ssrc))(sdst, ssrc),
                     inc=sS, waits=cw)
                ev[(wname, "cvs", idx)] = sS.n
                ev[(wname, "cv", idx)] = None
        ev[(wname, "conv_v", ofG)] = ev[(wname, "cv", ofG * 2)]
        ev[(wname, "conv_s", ofG)] = ev[(wname, "cvs", ofG * 2 + 1)]

    def emit_proj(wname, w_dram, bias_t, kind):
        emit_strip_ld(wname, w_dram, 0)
        emit_strip_cv(wname, 0)
        for ofG in range(16):
            if ofG + 1 < 16:
                emit_strip_ld(wname, w_dram, ofG + 1)
            buf = ofG % 2
            strip = BIGW.ap()[:, buf * 8192:buf * 8192 + 8192]
            for ofc2 in range(2):
                ofc = ofG * 2 + ofc2
                h = ofc // 2
                for th in range(2):
                    gi = (ofc * 2 + th) % 4
                    bank = PS[gi]
                    w0 = [(sV, ev[(wname, "conv_v", ofG)]),
                          (sS, ev[(wname, "conv_s", ofG)]),
                          (sV, ev[("h1T_half", th)]),
                          (sS, ev.get(("bankevac", gi))),
                          (dCst, ev["consts_loaded"])]
                    for cc in range(32):
                        lhs = strip[:, cc * 256 + ofc2 * 128:cc * 256 + ofc2 * 128 + 128]
                        rhs = h1T[:, cc * 1024 + th * 512:cc * 1024 + th * 512 + 512]
                        E.op("tensor",
                             (lambda bank, lhs, rhs, cc: lambda e: e.matmul(
                                 bank[:], lhs, rhs,
                                 start=(cc == 0), stop=(cc == 31)))(bank, lhs, rhs, cc),
                             inc=(sT if cc == 31 else None),
                             waits=(w0 if cc == 0 else []))
                    ev[(wname, "mm", ofc, th)] = sT.n
                    if ofc2 == 1 and th == 1:
                        ev[(wname, "stripuse", ofG)] = sT.n
                    if kind == "k" or kind == "v":
                        dst = SLOT.ap()[:, (ofc % 2) * 1024 + th * 512:
                                        (ofc % 2) * 1024 + th * 512 + 512]
                        wz = [(sT, ev[(wname, "mm", ofc, th)])]
                        if kind == "k":
                            kv_ = ev.get(("kstg_dma", ofc - 2))
                            if kv_ is not None:
                                wz.append(kv_)
                        else:
                            wz.append((sT, ev.get(("vtr", ofc - 2))))
                            if ofc < 2:
                                wz.append((dKs[0], ev["ck_done_0"]))
                                wz.append((dKs[1], ev["ck_done_1"]))
                    else:
                        dst = qT[:, ofc * 1024 + th * 512:ofc * 1024 + th * 512 + 512]
                        wz = [(sT, ev[(wname, "mm", ofc, th)]),
                              (dCst, ev["bdg_written"])]
                    E.op("scalar",
                         (lambda dst, bank, ofc: lambda e: e.activation(
                             dst, bank[:], AF.Identity,
                             bias=bias_t[:, ofc:ofc + 1]))(dst, bank, ofc),
                         inc=sS, waits=wz)
                    ev[("bankevac", gi)] = sS.n
                    ev[(wname, "evac", ofc, th)] = sS.n
                if kind == "k":
                    src = SLOT.ap()[:, (ofc % 2) * 1024:(ofc % 2) * 1024 + 1024]
                    E.op("sync",
                         (lambda src, ofc: lambda e: e.dma_start(
                             ck_d.ap()[ofc * 128:ofc * 128 + 128, :], src))(src, ofc),
                         inc=dKs[ofc % 2], inc_n=16,
                         waits=[(sS, ev[(wname, "evac", ofc, 1)])])
                    ev[("kstg_dma", ofc)] = (dKs[ofc % 2], dKs[ofc % 2].n)
                elif kind == "v":
                    bank = PS[6 + ofc % 2]
                    bankbf = bank.ap().bitcast(BF16)
                    wv = [(sS, ev[(wname, "evac", ofc, 1)]),
                          (sV, ev.get(("vtrev", ofc - 2)))]
                    for k in range(8):
                        src = SLOT.ap()[:, (ofc % 2) * 1024 + k * 128:
                                        (ofc % 2) * 1024 + k * 128 + 128]
                        E.op("tensor",
                             (lambda bankbf, src, k: lambda e: e.transpose(
                                 bankbf[:, k * 128:k * 128 + 128], src, EYE))(bankbf, src, k),
                             inc=(sT if k == 7 else None), waits=(wv if k == 0 else []))
                    ev[("vtr", ofc)] = sT.n
                    vstg = SLOT.ap()[:, 2048 + (ofc % 2) * 1024:
                                     2048 + (ofc % 2) * 1024 + 1024]
                    E.op("vector",
                         (lambda vstg, bankbf: lambda e: e.tensor_copy(
                             vstg, bankbf[:, 0:1024]))(vstg, bankbf),
                         inc=sV, waits=[(sT, ev[("vtr", ofc)])] +
                         ([ev[("vstg_dma", ofc - 2)]] if ("vstg_dma", ofc - 2) in ev else []))
                    ev[("vtrev", ofc)] = sV.n
                    E.op("sync",
                         (lambda vstg, ofc: lambda e: e.dma_start(
                             cv_d.ap()[:, ofc * 128:ofc * 128 + 128]
                             .rearrange("(a p) c -> p a c", p=128),
                             vstg.rearrange("p (a c) -> p a c", a=8)))(vstg, ofc),
                         inc=dVs[ofc % 2], inc_n=16, waits=[(sV, ev[("vtrev", ofc)])])
                    ev[("vstg_dma", ofc)] = (dVs[ofc % 2], dVs[ofc % 2].n)
                else:
                    for th in range(2):
                        bank = PS[4 + th]
                        src = qT[:, ofc * 1024 + th * 512:ofc * 1024 + th * 512 + 512]
                        E.op("tensor",
                             (lambda bank, h, src: lambda e: e.matmul(
                                 bank[:], BDIAG[:, h * 128:h * 128 + 128], src,
                                 start=True, stop=True))(bank, h, src),
                             inc=sT,
                             waits=[(sS, ev[(wname, "evac", ofc, th)]),
                                    (dCst, ev["bdiag_ready"]),
                                    (sS, ev.get(("bdq_ev", ofc - 1, th)))])
                        ev[("bdq_mm", ofc, th)] = sT.n
                        E.op("scalar",
                             (lambda src, bank: lambda e: e.activation(
                                 src, bank[:], AF.Identity))(src, bank),
                             inc=sS, waits=[(sT, ev[("bdq_mm", ofc, th)])])
                        ev[("bdq_ev", ofc, th)] = sS.n
            if ofG + 1 < 16:
                emit_strip_cv(wname, ofG + 1)

    emit_proj("wk", wkT_d, BK, "k")
    ev["ck_done_0"] = dKs[0].n
    ev["ck_done_1"] = dKs[1].n
    E.op("gpsimd",
         lambda e: e.collective_compute(
             "AllGather", ALU.bypass,
             replica_groups=[[0, 1, 2, 3], [4, 5, 6, 7]],
             ins=[ck_d.ap().opt()], outs=[gk_d.ap().opt()]),
         inc=CC, waits=[(dKs[0], ev["ck_done_0"]), (dKs[1], ev["ck_done_1"])])
    if phases <= 2:
        return _finish()
    emit_proj("wv", wvT_d, BV, "v")
    ev["cv_done_0"] = dVs[0].n
    ev["cv_done_1"] = dVs[1].n
    E.op("gpsimd",
         lambda e: e.collective_compute(
             "AllGather", ALU.bypass,
             replica_groups=[[0, 1, 2, 3], [4, 5, 6, 7]],
             ins=[cv_d.ap().opt()], outs=[gv_d.ap().opt()]),
         inc=CC, waits=[(dVs[0], ev["cv_done_0"]), (dVs[1], ev["cv_done_1"]), (CC, 1)])
    if phases <= 3:
        return _finish()
    emit_proj("wq", wqT_d, BQ, "q")
    ev["qT_done"] = sS.n
    for b in range(2):
        E.op("vector",
             (lambda b: lambda e: e.memset(
                 BIGV.ap()[:, b * 8224 + 256:b * 8224 + 8224:257], 1.0))(b),
             inc=sV, waits=[(sV, ev[("wq", "conv_v", 15)]),
                            (sS, ev[("wq", "conv_s", 15)])])
    ev["vhat_ones"] = sV.n

    if phases <= 4:
        return _finish()
    def emit_head_loads(h):
        hb = h % 2
        ktbuf = BIGW.ap()[:, hb * 8192:hb * 8192 + 8192]
        wk_ = [(CC, 1), (sT, ev.get(("kt_use", h - 2)))]
        if h < 2:
            wk_.append((sT, ev[("wq", "stripuse", 14 + h)]))
        for dc in range(2):
            ktsrc = gk_d.ap().rearrange("(r hh dc p) t -> hh dc p r t",
                                        r=4, hh=16, dc=2, p=128)[h, dc]
            ktdst = ktbuf.rearrange("p (dc r t) -> p dc r t", dc=2, r=4)[:, dc]
            E.op("sync",
                 (lambda ktdst, ktsrc: lambda e: e.dma_start(ktdst, ktsrc))(ktdst, ktsrc),
                 inc=dKT[hb], inc_n=16, waits=(wk_ if dc == 0 else []))
        ev[("kt_ld", h)] = (dKT[hb], dKT[hb].n)
        vb = BIGV.ap()[:, hb * 8224:hb * 8224 + 8224]
        vdst = vb.rearrange("p (mc j) -> p mc j", j=257)[:, :, 0:256]
        vsrc = gv_d.ap().rearrange("(a p) c -> p a c", p=128)[:, :, h * 256:h * 256 + 256]
        wv_ = [(CC, 2), (sV, ev["vhat_ones"]),
               (sT, ev.get(("v_use", h - 2)))]
        if h < 2:
            wv_ += [(sV, ev[("wq", "conv_v", 15)]), (sS, ev[("wq", "conv_s", 15)])]
        E.op("sync",
             (lambda vdst, vsrc: lambda e: e.dma_start(vdst, vsrc))(vdst, vsrc),
             inc=dVh[hb], inc_n=16, waits=wv_)
        ev[("v_ld", h)] = (dVh[hb], dVh[hb].n)

    emit_head_loads(0)
    for h in range(NH):
        hb = h % 2
        ktbuf = BIGW.ap()[:, hb * 8192:hb * 8192 + 8192]
        vb = BIGV.ap()[:, hb * 8224:hb * 8224 + 8224]
        if h + 1 < NH:
            emit_head_loads(h + 1)

        for tg in range(2):
            for mp in range(17):
                for mc in (2 * mp, 2 * mp + 1):
                    if mc >= 32:
                        continue
                    lb = PS[4 + mc % 4]
                    wl = []
                    if mc % 2 == 0:
                        wl = [ev[("kt_ld", h)],
                              (sS, ev.get(("exp", h, tg, mc - 3),
                                          ev.get(("exp", h, tg - 1, 29),
                                                 ev.get(("exp", h - 1, 1, 29))))),
                              (sS, ev["qT_done"])]
                    for dc in range(2):
                        lhs = ktbuf[:, dc * 4096 + mc * 128:dc * 4096 + mc * 128 + 128]
                        rhs = qT[:, (2 * h + dc) * 1024 + tg * 512:
                                 (2 * h + dc) * 1024 + tg * 512 + 512]
                        E.op("tensor",
                             (lambda lb, lhs, rhs, dc: lambda e: e.matmul(
                                 lb[:], lhs, rhs,
                                 start=(dc == 0), stop=(dc == 1)))(lb, lhs, rhs, dc),
                             inc=(sT if dc == 1 else None),
                             waits=(wl if dc == 0 else []))
                    ev[("log", h, tg, mc)] = sT.n
                    pt = SLOT.ap()[:, (mc % 4) * 512:(mc % 4) * 512 + 512]
                    we = [(sT, ev[("log", h, tg, mc)]),
                          (sT, ev.get(("pv", h, tg, mc - 4),
                                      ev.get(("pv", h, tg - 1, 31),
                                             ev.get(("pv", h - 1, 1, 31))))),
                          (sV, ev["cmemset"])]
                    if h == 0 and tg == 0 and mc < 2:
                        we.append(ev[("vstg_dma", 31)])
                    E.op("scalar",
                         (lambda pt, lb: lambda e: e.activation(
                             pt, lb[:], AF.Exp, bias=-SOFTMAX_C))(pt, lb),
                         inc=sS, waits=we)
                    ev[("exp", h, tg, mc)] = sS.n
                for pv in (2 * mp - 2, 2 * mp - 1):
                    if not (0 <= pv < 32):
                        continue
                    pt = SLOT.ap()[:, (pv % 4) * 512:(pv % 4) * 512 + 512]
                    wp = []
                    if pv % 2 == 0:
                        wp = [(sS, ev[("exp", h, tg, pv + 1)]),
                              ev[("v_ld", h)]]
                        if pv == 0:
                            nrm = ev.get(("norm", h, tg - 1), ev.get(("norm", h - 1, 1)))
                            if nrm is not None:
                                wp.append((sS, nrm[1]))
                    for t4 in range(4):
                        bank = PS[t4]
                        lhs = pt[:, t4 * 128:t4 * 128 + 128]
                        rhs = vb[:, pv * 257:pv * 257 + 257]
                        E.op("tensor",
                             (lambda bank, lhs, rhs, pv: lambda e: e.matmul(
                                 bank[:, 0:257], lhs, rhs,
                                 start=(pv == 0), stop=(pv == 31)))(bank, lhs, rhs, pv),
                             inc=(sT if t4 == 3 else None),
                             waits=(wp if t4 == 0 else []))
                    ev[("pv", h, tg, pv)] = sT.n
            ev[("kt_use", h)] = sT.n
            ev[("v_use", h)] = sT.n
            for t4 in range(4):
                bank = PS[t4]
                rs = SM[:, 58 + t4:59 + t4]
                E.op("vector",
                     (lambda bank, rs: lambda e: e.reciprocal(
                         rs, bank[:, 256:257]))(bank, rs),
                     inc=(sV if t4 == 3 else None),
                     waits=([(sT, ev[("pv", h, tg, 31)])] +
                            ([ev[("ao_dma", h - 2)]] if ("ao_dma", h - 2) in ev else [])
                            if t4 == 0 else []))
            vv = sV.n
            for t4 in range(4):
                bank = PS[t4]
                rs = SM[:, 58 + t4:59 + t4]
                dst = BIGA.ap()[:, (h % 2) * 2048 + (tg * 4 + t4) * 256:
                                (h % 2) * 2048 + (tg * 4 + t4) * 256 + 256]
                E.op("scalar",
                     (lambda dst, bank, rs: lambda e: e.activation(
                         dst, bank[:, 0:256], AF.Identity, scale=rs))(dst, bank, rs),
                     inc=sS, waits=[(sV, vv)])
            ev[("norm", h, tg)] = ("S", sS.n)
        E.op("sync",
             (lambda h: lambda e: e.dma_start(
                 ao_d.ap()[:, h * 256:h * 256 + 256]
                 .rearrange("(a p) c -> p a c", p=128),
                 BIGA.ap()[:, (h % 2) * 2048:(h % 2) * 2048 + 2048]
                 .rearrange("p (a c) -> p a c", a=8)))(h),
             inc=dAo[h % 2], inc_n=16, waits=[(sS, ev[("norm", h, 1)][1])])
        ev[("ao_dma", h)] = (dAo[h % 2], dAo[h % 2].n)
    ev["attn_done"] = [(dAo[0], dAo[0].n), (dAo[1], dAo[1].n)]

    if debug:
        E.op("sync", lambda e: e.dma_start(dbg_a.ap(), BIGA.ap()[:]),
             inc=dDbg, inc_n=16, waits=[(sS, ev["qT_done"])])
        E.op("sync", lambda e: e.dma_start(dbg_b.ap(), BIGB.ap()[:]),
             inc=dDbg, inc_n=16, waits=[(sT, ev[("kt_use", NH - 1)])])
        E.op("sync", lambda e: e.dma_start(dbg_gk.ap(), gk_d.ap()),
             inc=dDbg, inc_n=16, waits=[(CC, 2)])
        E.op("sync", lambda e: e.dma_start(dbg_gv.ap(), gv_d.ap()),
             inc=dDbg, inc_n=16)
        E.op("sync", lambda e: e.dma_start(dbg_ao.ap(), ao_d.ap()),
             inc=dDbg, inc_n=16, waits=list(ev["attn_done"]))
        E.op("sync", lambda e: e.dma_start(dbg_bd.ap(), bdg_d.ap()),
             inc=dDbg, inc_n=16, waits=[(dCst, ev["bdg_written"])])
        ev["dbg_done"] = dDbg.n

    if phases <= 5:
        return _finish()
    ev["_dq_early"] = True
    E.op("sync", lambda e: e.dma_start(EYE, eye_d.ap()),
         inc=dCst, inc_n=16,
         waits=[(sT, ev[("pv", NH - 1, 1, 31)]),
                (sS, ev[("exp", NH - 1, 1, 31)])])
    ev["eye2"] = dCst.n
    for ts in range(NTS):
        par = ts % 2
        xt = BIGVf[:, par * 4096:par * 4096 + 4096]
        aot = BIGW.ap()[:, par * 4096:par * 4096 + 4096]
        _x2p = ev.get(("p6", "x2", ts - 2))
        wx6 = list(ev["attn_done"]) + [
               (CC, _x2p[1]) if _x2p else None,
               (sT, ev[("v_use", NH - 1)])]
        wx6 = [w_ for w_ in wx6 if w_ is not None]
        if debug:
            wx6.append((dDbg, ev["dbg_done"]))
        E.op("scalar",
             (lambda xt, ts: lambda e: e.dma_start(
                 xt, x_d.ap()[ts * 128:ts * 128 + 128, :]))(xt, ts),
             inc=dP6[par], inc_n=16, waits=wx6)
        E.op("scalar",
             (lambda aot, ts: lambda e: e.dma_start(
                 aot, ao_d.ap()[ts * 128:ts * 128 + 128, :]))(aot, ts),
             inc=dP6[par], inc_n=16,
             waits=[(sT, ev[("kt_use", NH - 1)])])
        ev[("p6", "ld", ts)] = (dP6[par], dP6[par].n)
        x2sl = x2res[:, ts * 4096:ts * 4096 + 4096]
        E.op("gpsimd",
             (lambda x2sl, xt, aot: lambda e: e.tensor_tensor(
                 x2sl, xt, aot, ALU.add))(x2sl, xt, aot),
             inc=CC, waits=[ev[("p6", "ld", ts)],
                            (CC, 2 + ts)])
        ev[("p6", "x2", ts)] = ("CCg", CC.n)
        h2tile = BIGW.ap()[:, 8192 + par * 4096:8192 + par * 4096 + 4096]
        emit_ln(x2sl, ts, h2tile, "p6")
        for g in range(4):
            gg = ts * 4 + g
            bank = PS[6 + gg % 2]
            bankbf = bank.ap().bitcast(BF16)
            pr = g // 2
            w = [(sS, ev[("p6", "h1", ts)]),
                 (sV, ev.get(("p6trev", gg - 2))),
                 (dCst, ev["eye2"])]
            for k in range(8):
                jc = (g % 2) * 8 + k
                src_ = h2tile[:, jc * 256 + pr:jc * 256 + pr + 255:2]
                E.op("tensor",
                     (lambda bankbf, src_, k: lambda e: e.transpose(
                         bankbf[:, k * 128:k * 128 + 128], src_, EYE))(bankbf, src_, k),
                     inc=(sT if k == 7 else None), waits=(w if k == 0 else []))
            ev[("p6tr", gg)] = sT.n
            st = pr * 16 + (g % 2) * 8
            dst = BIGA.ap().rearrange("p (a t) -> p a t", t=1024)[
                :, st:st + 8, ts * 128:ts * 128 + 128]
            E.op("vector",
                 (lambda dst, bankbf: lambda e: e.tensor_copy(
                     dst, bankbf[:, 0:1024].rearrange("p (a t) -> p a t", a=8)))(dst, bankbf),
                 inc=sV, waits=[(sT, ev[("p6tr", gg)])])
            ev[("p6trev", gg)] = sV.n
        ev[("p6", "trdone", ts)] = sT.n
    ev["h2T_done"] = sV.n

    if phases <= 6:
        return _finish()
    wrow = [(sT, ev[("v_use", NH - 1)]), (sV, ev["h2T_done"]),
            (CC, ev[("p6", "x2", NTS - 1)][1])]
    E.op("vector", lambda e: e.memset(ONES_ROW, 1.0), inc=sV, waits=wrow)
    ev["ones_row"] = sV.n
    ROWBUF = [(SMALL2.ap()[0:1, 0:512], SMALL2.ap()[0:1, 512:1024]),
              (BIGVf[0:1, 6656:7168], SLOT.ap().bitcast(F32)[0:1, 0:512])]
    dRow2 = [dRow, dRowB]
    for i in range(8):
        sch0, zch0 = ROWBUF[i % 2]
        dr = dRow2[i % 2]
        E.op("sync",
             (lambda sch0, i: lambda e: e.dma_start(
                 sch0, fs_d.ap()[:, i * 512:i * 512 + 512]))(sch0, i),
             inc=dr, inc_n=16,
             waits=[(sV, ev.get(("rowc", i - 2), ev["ones_row"]))])
        E.op("sync",
             (lambda zch0, i: lambda e: e.dma_start(
                 zch0, fz_d.ap()[:, i * 512:i * 512 + 512]))(zch0, i),
             inc=dr, inc_n=16)
        lvz = (dr, dr.n)
        E.op("vector",
             (lambda sch0, i: lambda e: e.tensor_copy(
                 SROWBF[:, i * 512:i * 512 + 512], sch0))(sch0, i),
             inc=sV, waits=[lvz])
        E.op("vector",
             (lambda zch0, i: lambda e: e.tensor_scalar_mul(
                 ZNEG[:, i * 512:i * 512 + 512], zch0, -1.0))(zch0, i),
             inc=sV, waits=[(sV, sV.n)])
        ev[("rowc", i)] = sV.n
    ev["rows_ld"] = dRow.n
    ev["zneg"] = sV.n
    if phases == 61:
        return _finish()
    BOSBUF = [(SMALL2.ap()[0:1, 0:512], SMALL2.ap()[0:1, 512:1024],
               SLOT.ap().bitcast(F32)[0:1, 512:1024]),
              (BIGVf[0:1, 6656:7168], SLOT.ap().bitcast(F32)[0:1, 0:512],
               SLOT.ap().bitcast(F32)[0:1, 1536:2048])]
    for i in range(8):
        sch, rch, bch = BOSBUF[i % 2]
        dr = dRow2[i % 2]
        E.op("sync",
             (lambda sch, i: lambda e: e.dma_start(
                 sch, fs_d.ap()[:, i * 512:i * 512 + 512]))(sch, i),
             inc=dr, inc_n=16,
             waits=[(sV, ev.get(("bos", i - 2), ev["zneg"]))])
        E.op("sync",
             (lambda bch, i: lambda e: e.dma_start(
                 bch, fb_d.ap()[:, i * 512:i * 512 + 512]))(bch, i),
             inc=dr, inc_n=16)
        lv = (dr, dr.n)
        E.op("vector",
             (lambda rch, sch: lambda e: e.reciprocal(rch, sch))(rch, sch),
             inc=sV, waits=[lv])
        vv = sV.n
        E.op("vector",
             (lambda rch, bch, i: lambda e: e.tensor_tensor(
                 BOS[:, i * 512:i * 512 + 512], bch, rch, ALU.mult))(rch, bch, i),
             inc=sV, waits=[(sV, vv)])
        ev[("bos", i)] = sV.n
    ev["bos_done"] = sV.n
    if phases == 62:
        return _finish()
    sbc = BIGWf[:, 0:4096]
    for i in range(8):
        bank = PS[4 + i % 2]
        E.op("tensor",
             (lambda i, bank: lambda e: e.matmul(
                 bank[:], ONES_ROW, SROWBF[:, i * 512:i * 512 + 512],
                 start=True, stop=True))(i, bank),
             inc=sT, waits=[(sV, ev["zneg"]),
                            ((sS, ev[("sbc", i - 2)][1]) if ("sbc", i - 2) in ev
                             else (sV, ev["ones_row"])),
                            (sS, ev[("exp", NH - 1, 1, 31)])])
        tv_ = sT.n
        E.op("scalar",
             (lambda i, bank: lambda e: e.copy(
                 sbc[:, i * 512:i * 512 + 512], bank[:]))(i, bank),
             inc=sS, waits=[(sT, tv_), (CC, ev[("p6", "x2", NTS - 1)][1])])
        ev[("sbc", i)] = ("S", sS.n)
    ev["sbc_done"] = ("S", sS.n)
    if phases == 63:
        return _finish()
    for th in range(2):
        bank = PS[6 + th]
        for idx in range(32):
            jc, pr = idx % 16, idx // 16
            rhs = BIGA.ap()[:, pr * 16384 + jc * 1024 + th * 512:
                            pr * 16384 + jc * 1024 + th * 512 + 512]
            E.op("tensor",
                 (lambda rhs, idx, bank: lambda e: e.matmul(
                     bank[0:1, :], ONES128, rhs,
                     start=(idx == 0), stop=(idx == 31)))(rhs, idx, bank),
                 inc=(sT if idx == 31 else None),
                 waits=([(sV, ev["h2T_done"]),
                         (sV, ev.get(("p6trev", NTS * 4 - 1 - th)))] if idx == 0 else []))
        ev[("srow_mm", th)] = sT.n
        E.op("vector",
             (lambda th, bank: lambda e: e.tensor_copy(
                 S_ROW[:, th * 512:th * 512 + 512], bank[0:1, :]))(th, bank),
             inc=sV, waits=[(sT, ev[("srow_mm", th)])])
        ev[("srow_ev", th)] = sV.n
    ev["srow_done"] = sV.n

    if phases <= 7:
        return _finish()
    if debug:
        E.op("sync", lambda e: e.dma_start(dbg_h2.ap(), BIGA.ap()[:]),
             inc=dDbg, inc_n=16, waits=[(sV, ev["h2T_done"])])
        E.op("sync", lambda e: e.dma_start(dbg_x2.ap(), BIGB.ap()[:]),
             inc=dDbg, inc_n=16)
        E.op("sync", lambda e: e.dma_start(dbg_rw.ap(), BIGV.ap().bitcast(I32)[0:1, 0:7232]),
             inc=dDbg, inc_n=16,
             waits=[(sV, ev["srow_done"]), (sV, ev["bos_done"]), (sV, ev["zneg"])])
        E.op("sync", lambda e: e.dma_start(dbg_sb.ap(), BIGWf[:, 0:4096]),
             inc=dDbg, inc_n=16, waits=[(sS, ev["sbc_done"][1])])
        ev["dbg2_done"] = dDbg.n

    dq_done = set()

    def emit_dq(ofb, jc):
        if (ofb, jc) in dq_done or ofb >= 8:
            return
        dq_done.add((ofb, jc))
        par = jc % 2
        qb = BIGWi[:, 4096 + par * 512:4096 + par * 512 + 512]
        ih = BIGWi[:, 5120 + par * 1024:5120 + par * 1024 + 512]
        il = BIGWi[:, 5120 + par * 1024 + 512:5120 + par * 1024 + 1024]
        nh_ = BIGW.ap()[:, 14336 + par * 1024:14336 + par * 1024 + 512]
        nl_ = BIGW.ap()[:, 14336 + par * 1024 + 512:14336 + par * 1024 + 1024]
        gidx = ofb * 16 + jc
        E.op("gpsimd",
             (lambda qb, ofb, jc: lambda e: e.dma_start(
                 qb, qwT_d.ap()[jc * 128:jc * 128 + 128,
                                ofb * 512:ofb * 512 + 512]))(qb, ofb, jc),
             inc=dQw[par], inc_n=16,
             waits=[((sV, ev[("unp", gidx - 2)]) if ("unp", gidx - 2) in ev
                     else None),
                    (sT, ev.get(("p6", "trdone", NTS - 1))),
                    (sT, ev[("kt_use", NH - 1)])])
        lv = (dQw[par], dQw[par].n)
        E.op("vector",
             (lambda ih, qb: lambda e: e.tensor_scalar(
                 ih, qb, 4, 15, ALU.logical_shift_right, ALU.bitwise_and))(ih, qb),
             inc=sV, waits=[lv, (sS, ev.get(("cvt", gidx - 2)))])
        E.op("vector",
             (lambda il, qb: lambda e: e.tensor_scalar(
                 il, qb, 15, None, ALU.bitwise_and))(il, qb),
             inc=sV, waits=[(sV, sV.n)])
        ev[("unp", gidx)] = sV.n
        wc = [(sV, ev[("unp", gidx)]),
              (sT, ev.get(("nibuse", gidx - 2)))]
        E.op("scalar",
             (lambda nh_, ih: lambda e: e.activation(nh_, ih, AF.Copy))(nh_, ih),
             inc=sS, waits=wc)
        E.op("scalar",
             (lambda nl_, il: lambda e: e.activation(nl_, il, AF.Copy))(nl_, il),
             inc=sS, waits=[(sS, sS.n)])
        ev[("cvt", gidx)] = sS.n

    emit_dq(0, 0)
    emit_dq(0, 1)
    for ofb in range(8):
        for jc in range(16):
            emit_dq(ofb, jc)
            par = jc % 2
            gidx = ofb * 16 + jc
            nh_ = BIGW.ap()[:, 14336 + par * 1024:14336 + par * 1024 + 512]
            nl_ = BIGW.ap()[:, 14336 + par * 1024 + 512:14336 + par * 1024 + 1024]
            for pr, nib in ((0, nh_), (1, nl_)):
                for ts in range(NTS):
                    bank = PS[ts]
                    lhs = BIGA.ap()[:, pr * 16384 + jc * 1024 + ts * 128:
                                    pr * 16384 + jc * 1024 + ts * 128 + 128]
                    w0 = []
                    if ts == 0:
                        w0.append((sS, ev[("cvt", gidx)]))
                    if jc == 0 and pr == 0:
                        w0.append((sV, ev.get(("epi", ofb - 1, ts))))
                        if ofb == 0:
                            w0 += [(sV, ev["srow_done"]), (sV, ev["bos_done"]),
                                   (sV, ev["zneg"])]
                    E.op("tensor",
                         (lambda bank, lhs, nib, jc, pr: lambda e: e.matmul(
                             bank[:], lhs, nib,
                             start=(jc == 0 and pr == 0), stop=False))(bank, lhs, nib, jc, pr),
                         inc=(sT if (pr == 1 and ts == NTS - 1) else None),
                         waits=w0)
            ev[("nibuse", gidx)] = sT.n
        emit_dq(ofb + 1, 0)
        emit_dq(ofb + 1, 1)
        for ts in range(NTS):
            bank = PS[ts]
            if phases != 81:
                E.op("tensor",
                     (lambda bank, ts, ofb: lambda e: e.matmul(
                         bank[:], S_ROW[:, ts * 128:ts * 128 + 128],
                         ZNEG[:, ofb * 512:ofb * 512 + 512],
                         start=False, stop=False))(bank, ts, ofb))
                E.op("tensor",
                     (lambda bank, ofb: lambda e: e.matmul(
                         bank[:], ONES_ROW,
                         BOS[:, ofb * 512:ofb * 512 + 512],
                         start=False, stop=True))(bank, ofb),
                     inc=sT)
            else:
                E.op("tensor",
                     (lambda bank, ofb: lambda e: e.matmul(
                         bank[:], ONES_ROW,
                         BOS[:, ofb * 512:ofb * 512 + 512],
                         start=False, stop=True))(bank, ofb),
                     inc=sT)
            ev[("ffn_mm", ofb, ts)] = sT.n
            ob = ts % 2
            t1 = SLOT.ap().bitcast(F32)[:, 1024 + ob * 512:1024 + ob * 512 + 512]
            o1 = SMALL2.ap()[:, ob * 512:ob * 512 + 512]
            E.op("vector",
                 (lambda t1, bank, ofb: lambda e: e.tensor_tensor(
                     t1, bank[:], sbc[:, ofb * 512:ofb * 512 + 512],
                     ALU.mult))(t1, bank, ofb),
                 inc=sV, waits=[(sT, ev[("ffn_mm", ofb, ts)]),
                                (sV, ev.get(("epi", ofb, ts - 2),
                                            ev.get(("epi", ofb - 1, NTS - 2 + ob))))])
            ev[("epi_t1", ofb, ts)] = sV.n
            E.op("vector",
                 (lambda o1, t1, ofb, ts: lambda e: e.tensor_tensor(
                     o1, t1, x2res[:, ts * 4096 + ofb * 512:ts * 4096 + ofb * 512 + 512],
                     ALU.add))(o1, t1, ofb, ts),
                 inc=sV, waits=[(sV, ev[("epi_t1", ofb, ts)])] +
                 ([ev[("out_dma", ofb, ts - 2)]] if ("out_dma", ofb, ts - 2) in ev
                  else ([ev[("out_dma", ofb - 1, ts + 6)]]
                        if ("out_dma", ofb - 1, ts + 6) in ev else [])))
            ev[("epi", ofb, ts)] = sV.n
            E.op("gpsimd",
                 (lambda o1, ofb, ts: lambda e: e.dma_start(
                     out_d.ap()[ts * 128:ts * 128 + 128,
                                ofb * 512:ofb * 512 + 512], o1))(o1, ofb, ts),
                 inc=dOut[ob], inc_n=16, waits=[(sV, ev[("epi", ofb, ts)])])
            ev[("out_dma", ofb, ts)] = (dOut[ob], dOut[ob].n)

    if phases <= 7:
        return _finish()
    return _finish()


_NC_CACHE = None


def kernel(L0_attn_x, L0_attn_Wq, L0_attn_bq, L0_attn_Wk, L0_attn_bk,
           L0_attn_Wv, L0_attn_bv, L0_attn_bias_weight, L0_ffn_qweight,
           L0_ffn_scale, L0_ffn_zero, L0_ffn_bias):
    global _NC_CACHE
    f32 = np.float32
    x = np.ascontiguousarray(np.asarray(L0_attn_x, f32).reshape(2 * 4096, DM))
    def _halfstrip(w):
        wT = np.asarray(w, f32).T
        return np.ascontiguousarray(
            wT.reshape(32, 128, 32, 128).transpose(2, 1, 0, 3).reshape(4096, 4096))
    wqT = _halfstrip(L0_attn_Wq)
    wkT = _halfstrip(L0_attn_Wk)
    wvT = _halfstrip(L0_attn_Wv)
    qwT = np.ascontiguousarray(np.asarray(L0_ffn_qweight, np.int32).T)
    bq_t = np.ascontiguousarray(np.asarray(L0_attn_bq, f32).reshape(32, 128).T)
    bk_t = np.ascontiguousarray(np.asarray(L0_attn_bk, f32).reshape(32, 128).T)
    bv_t = np.ascontiguousarray(np.asarray(L0_attn_bv, f32).reshape(32, 128).T)
    bwT = np.ascontiguousarray(np.asarray(L0_attn_bias_weight, f32).T)
    fs = np.ascontiguousarray(np.asarray(L0_ffn_scale, f32).reshape(1, DM))
    fz = np.ascontiguousarray(np.asarray(L0_ffn_zero, f32).reshape(1, DM))
    fb = np.ascontiguousarray(np.asarray(L0_ffn_bias, f32).reshape(1, DM))
    bdc4, eye = _host_consts()

    if _NC_CACHE is None:
        _NC_CACHE = build_nc()
    nc = _NC_CACHE
    in_maps = []
    for c in range(NCORES):
        in_maps.append({
            "x": np.ascontiguousarray(x[c * R:(c + 1) * R]),
            "wqT": wqT, "wkT": wkT, "wvT": wvT, "qwT": qwT,
            "bq_t": bq_t, "bk_t": bk_t, "bv_t": bv_t, "bwT": bwT,
            "fscale": fs, "fzero": fz, "fbias": fb,
            "eye": eye, "bdc": bdc4,
        })
    res = run_bass_kernel_spmd(nc, in_maps, core_ids=list(range(NCORES)))
    out = np.concatenate([np.asarray(r["out"]) for r in res.results], axis=0)
    return out.reshape(2, 4096, DM)


# version 12 (final)
# speedup vs baseline: 1.0326x; 1.0326x over previous
import numpy as np
import ml_dtypes
import concourse.bass as bass
import concourse.mybir as mybir
from concourse.bass_utils import run_bass_kernel_spmd

F32 = mybir.dt.float32
BF16 = mybir.dt.bfloat16
I32 = mybir.dt.int32
AF = mybir.ActivationFunctionType
ALU = mybir.AluOpType

NCORES = 8
R = 1024
NTS = 8
DM = 4096
NH = 16
SCALE = 1.0 / 16.0
SOFTMAX_C = 10.0
LN_EPS = 1e-5


def _build_tables():
    g = np.array([1.0, 1.0, 1.0], dtype=np.float32)
    dim = len(g)
    n = 2 ** dim
    masks = sorted(range(n), key=lambda m: (bin(m).count('1'), m))
    inv = {m: i for i, m in enumerate(masks)}
    C = np.zeros((n, n, n), dtype=np.float32)
    for i, a in enumerate(masks):
        for j, b in enumerate(masks):
            sw, at = (0, a >> 1)
            while at:
                sw += bin(at & b).count('1')
                at >>= 1
            s = -1.0 if sw & 1 else 1.0
            for t in range(dim):
                if a >> t & 1 and b >> t & 1:
                    s *= float(g[t])
            C[i, j, inv[a ^ b]] = s
    grades = np.array([bin(m).count('1') for m in masks])
    rev = np.where(grades * (grades - 1) // 2 % 2 == 1, -1.0, 1.0).astype(np.float32)
    met = np.array([float(np.prod([g[t] for t in range(dim) if m >> t & 1] + [1.0]))
                    for m in masks], dtype=np.float32)
    return C, rev, met


def _host_consts():
    C, rev, met = _build_tables()
    biv = C[:, :, [4, 5, 6]]
    bdc = np.zeros((4, 128, 128), dtype=np.float32)
    blk = rev[None, :, None] * biv.transpose(2, 0, 1)
    diag = SCALE * np.diag(rev * met)
    for d in range(16):
        sl = slice(d * 8, d * 8 + 8)
        for c in range(3):
            bdc[c][sl, sl] = blk[c]
        bdc[3][sl, sl] = diag
    return bdc.reshape(4, 128 * 128), np.eye(128).astype(ml_dtypes.bfloat16)


class Sem:
    def __init__(self, nc, name):
        self.h = nc.alloc_semaphore(name)
        self.n = 0


class Em:
    ENGS = ("sync", "gpsimd", "vector", "scalar", "tensor")

    def __init__(self):
        self.q = {e: [] for e in self.ENGS}
        self.last_wait = {e: {} for e in self.ENGS}

    def op(self, eng, fn, inc=None, inc_n=1, waits=()):
        ws = []
        for w_ in waits:
            if w_ is None:
                continue
            (s, v) = w_
            if s is None or v is None or v <= 0:
                continue
            key = id(s)
            if self.last_wait[eng].get(key, -1) >= v:
                continue
            self.last_wait[eng][key] = v
            ws.append((s.h, v))
        iv = None
        if inc is not None:
            inc.n += inc_n
            iv = (inc.h, inc_n)
        self.q[eng].append((ws, fn, iv))

    def wait(self, eng, waits):
        self.op(eng, None, waits=waits)

    def replay(self, eng_name, eng):
        for (ws, fn, iv) in self.q[eng_name]:
            for (sh, v) in ws:
                eng.wait_ge(sh, v)
            if fn is not None:
                ins = fn(eng)
                if iv is not None:
                    ins.then_inc(iv[0], iv[1])
            elif iv is not None:
                eng.sem_inc(iv[0], iv[1])


def build_nc(debug=False, phases=8):
    nc = bass.Bass("TRN2", target_bir_lowering=False)

    x_d = nc.dram_tensor("x", [R, DM], F32, kind="ExternalInput")
    wqT_d = nc.dram_tensor("wqT", [DM, DM], F32, kind="ExternalInput")
    wkT_d = nc.dram_tensor("wkT", [DM, DM], F32, kind="ExternalInput")
    wvT_d = nc.dram_tensor("wvT", [DM, DM], F32, kind="ExternalInput")
    qwT_d = nc.dram_tensor("qwT", [2048, DM], I32, kind="ExternalInput")
    bq_d = nc.dram_tensor("bq_t", [128, 32], F32, kind="ExternalInput")
    bk_d = nc.dram_tensor("bk_t", [128, 32], F32, kind="ExternalInput")
    bv_d = nc.dram_tensor("bv_t", [128, 32], F32, kind="ExternalInput")
    bwT_d = nc.dram_tensor("bwT", [3, 16], F32, kind="ExternalInput")
    fs_d = nc.dram_tensor("fscale", [1, DM], F32, kind="ExternalInput")
    fz_d = nc.dram_tensor("fzero", [1, DM], F32, kind="ExternalInput")
    fb_d = nc.dram_tensor("fbias", [1, DM], F32, kind="ExternalInput")
    eye_d = nc.dram_tensor("eye", [128, 128], BF16, kind="ExternalInput")
    bdc_d = nc.dram_tensor("bdc", [4, 16384], F32, kind="ExternalInput")
    out_d = nc.dram_tensor("out", [R, DM], F32, kind="ExternalOutput")

    ck_d = nc.dram_tensor("ck", [DM, R], BF16, kind="Internal")
    cv_d = nc.dram_tensor("cv", [R, DM], BF16, kind="Internal")
    gk_d = nc.dram_tensor("gk", [4 * DM, R], BF16, kind="Internal")
    gv_d = nc.dram_tensor("gv", [4 * R, DM], BF16, kind="Internal")
    ao_d = nc.dram_tensor("ao", [R, DM], BF16, kind="Internal")
    bdg_d = nc.dram_tensor("bdg", [16, 16384], BF16, kind="Internal")
    if debug:
        dbg_a = nc.dram_tensor("dbg_a", [128, 32768], BF16, kind="ExternalOutput")
        dbg_b = nc.dram_tensor("dbg_b", [128, 32768], BF16, kind="ExternalOutput")
        dbg_gk = nc.dram_tensor("dbg_gk", [4 * DM, R], BF16, kind="ExternalOutput")
        dbg_gv = nc.dram_tensor("dbg_gv", [4 * R, DM], BF16, kind="ExternalOutput")
        dbg_ao = nc.dram_tensor("dbg_ao", [R, DM], BF16, kind="ExternalOutput")
        dbg_bd = nc.dram_tensor("dbg_bd", [16, 16384], BF16, kind="ExternalOutput")
        dbg_h2 = nc.dram_tensor("dbg_h2", [128, 32768], BF16, kind="ExternalOutput")
        dbg_x2 = nc.dram_tensor("dbg_x2", [128, 32768], BF16, kind="ExternalOutput")
        dbg_rw = nc.dram_tensor("dbg_rw", [1, 7232], I32, kind="ExternalOutput")
        dbg_sb = nc.dram_tensor("dbg_sb", [128, 4096], F32, kind="ExternalOutput")

    BIGA = nc.alloc_sbuf_tensor("BIGA", [128, 32768], BF16)
    BIGB = nc.alloc_sbuf_tensor("BIGB", [128, 32768], BF16)
    BIGW = nc.alloc_sbuf_tensor("BIGW", [128, 16384], BF16)
    BIGV = nc.alloc_sbuf_tensor("BIGV", [128, 16448], BF16)
    SLOT = nc.alloc_sbuf_tensor("SLOT", [128, 4096], BF16)
    PTS = nc.alloc_sbuf_tensor("PTS", [128, 128], BF16)
    SMALL = nc.alloc_sbuf_tensor("SMALL", [128, 256], F32)
    SMALL2 = nc.alloc_sbuf_tensor("SMALL2", [128, 1024], F32)

    PS = [nc.alloc_psum_tensor("PS%d" % i, [128, 512], F32) for i in range(8)]
    EYE = PTS.ap()

    BIGWf = BIGW.ap().bitcast(F32)
    BIGWi = BIGW.ap().bitcast(I32)
    BIGVf = BIGV.ap().bitcast(F32)
    BIGBf = BIGB.ap().bitcast(F32)
    S2bf = SMALL2.ap().bitcast(BF16)

    h1T = BIGA.ap()
    qT = BIGB.ap()
    x2res = BIGB.ap()
    BDIAG = S2bf
    SM = SMALL.ap()
    BQ, BK, BV = SM[:, 64:96], SM[:, 96:128], SM[:, 128:160]
    BW4 = SM[0:4, 160:176]
    SMbf = SMALL.ap().bitcast(BF16)
    ONES128 = SMbf[:, 511:512]

    ZNEG = BIGV.ap()[0:1, 0:4096]
    BOS = BIGV.ap()[0:1, 4096:8192]
    SROWBF = BIGV.ap()[0:1, 8192:12288]
    S_ROW = BIGV.ap()[0:1, 12288:13312]
    ONES_ROW = BIGV.ap()[0:1, 14336:14464]

    dCst = Sem(nc, "dCst")
    dXT = [Sem(nc, "dXT0"), Sem(nc, "dXT1")]
    dSg = [Sem(nc, "dSg0"), Sem(nc, "dSg1")]
    dKs = [Sem(nc, "dKs0"), Sem(nc, "dKs1")]
    dVs = [Sem(nc, "dVs0"), Sem(nc, "dVs1")]
    dKT = [Sem(nc, "dKT0"), Sem(nc, "dKT1")]
    dVh = [Sem(nc, "dVh0"), Sem(nc, "dVh1")]
    dAo = [Sem(nc, "dAo0"), Sem(nc, "dAo1")]
    dP6 = [Sem(nc, "dP60"), Sem(nc, "dP61")]
    dRow = Sem(nc, "dRow")
    dRowB = Sem(nc, "dRowB")
    dQw = [Sem(nc, "dQw0"), Sem(nc, "dQw1")]
    dOut = [Sem(nc, "dOut0"), Sem(nc, "dOut1")]
    dDbg = Sem(nc, "dDbg")
    CC = Sem(nc, "CC")
    sV = Sem(nc, "sV")
    sS = Sem(nc, "sS")
    sT = Sem(nc, "sT")

    E = Em()
    ev = {}

    nc.const_aps.register(F32, -SOFTMAX_C, SM[:, 56:57])
    nc.const_aps.register(F32, -0.5, SM[:, 57:58])

    E.op("sync", lambda e: e.dma_start(EYE, eye_d.ap()), inc=dCst, inc_n=16)
    E.op("sync", lambda e: e.dma_start(BQ, bq_d.ap()), inc=dCst, inc_n=16)
    E.op("sync", lambda e: e.dma_start(BK, bk_d.ap()), inc=dCst, inc_n=16)
    E.op("sync", lambda e: e.dma_start(BV, bv_d.ap()), inc=dCst, inc_n=16)
    E.op("vector", lambda e: e.memset(SM[0:4, 160:176], 1.0), inc=sV)
    ev["bw4memset"] = sV.n
    E.op("vector", lambda e: e.memset(SM[:, 56:57], -SOFTMAX_C), inc=sV)
    E.op("vector", lambda e: e.memset(SM[:, 57:58], -0.5), inc=sV)
    ev["cmemset"] = sV.n
    E.op("vector", lambda e: e.memset(ONES128, 1.0), inc=sV)
    E.op("sync", lambda e: e.dma_start(SM[0:3, 160:176], bwT_d.ap()),
         inc=dCst, inc_n=16, waits=[(sV, ev["bw4memset"])])
    ev["consts_loaded"] = dCst.n

    for ch in range(32):
        bdct = BIGWf[0:4, (ch % 2) * 512:(ch % 2) * 512 + 512]
        E.op("sync",
             (lambda bdct, ch: lambda e: e.dma_start(
                 bdct, bdc_d.ap()[:, ch * 512:ch * 512 + 512]))(bdct, ch),
             inc=dSg[ch % 2], inc_n=16, waits=[(sT, ev.get(("bdg_mm", ch - 2)))])
        ldv = (dSg[ch % 2], dSg[ch % 2].n)
        E.op("tensor",
             (lambda bdct: lambda e: e.matmul(
                 PS[5][0:16, :], BW4, bdct, start=True, stop=True))(bdct),
             inc=sT, waits=[ldv, (sV, ev.get(("bdg_ev", ch - 1))),
                            (dCst, ev["consts_loaded"])])
        ev[("bdg_mm", ch)] = sT.n
        E.op("vector",
             (lambda ch: lambda e: e.tensor_copy(
                 BIGB.ap()[0:16, ch * 512:ch * 512 + 512], PS[5][0:16, :]))(ch),
             inc=sV, waits=[(sT, ev[("bdg_mm", ch)])])
        ev[("bdg_ev", ch)] = sV.n
    E.op("sync", lambda e: e.dma_start(bdg_d.ap(), BIGB.ap()[0:16, 0:16384]),
         inc=dCst, inc_n=16, waits=[(sV, ev[("bdg_ev", 31)])])
    ev["bdg_written"] = dCst.n

    def emit_ln(src, ts, h1dst, phase):
        lb = 192 * (ts % 2)
        bnst = SM[:, lb + 0:lb + 48]
        bnag = SM[:, lb + 48:lb + 50]
        tv = SM[:, lb + 50:lb + 51]
        tl = SM[:, lb + 51:lb + 52]
        rstd = SM[:, lb + 52:lb + 53]
        mrs = SM[:, lb + 53:lb + 54]
        for i in range(8):
            w = []
            if i == 0:
                w = [(sS, ev.get((phase, "lnuse", ts - 2)))]
                if phase == "p1":
                    w.append(ev[(phase, "ld", ts)])
                else:
                    t_ = ev[("p6", "x2", ts)]
                    w.append((CC, t_[1]) if isinstance(t_, tuple) and t_[0] == "CCg" else t_)
            E.op("vector",
                 (lambda i: lambda e: e.bn_stats(
                     SM[:, lb + 6 * i:lb + 6 * i + 6],
                     src[:, 512 * i:512 * i + 512]))(i),
                 inc=(sV if i == 7 else None), waits=w)
        ev[(phase, "bn", ts)] = sV.n
        E.op("vector", lambda e: e.bn_aggr(bnag, bnst),
             inc=sV, waits=[(sV, ev[(phase, "bn", ts)])])
        E.op("vector", lambda e: e.tensor_scalar(tv, SM[:, lb + 49:lb + 50], 1.0,
                                                 LN_EPS, ALU.mult, ALU.add),
             inc=sV, waits=[(sV, sV.n)])
        vv = sV.n
        E.op("scalar", lambda e: e.activation(tl, tv, AF.Ln),
             inc=sS, waits=[(sV, vv)])
        E.op("scalar", lambda e: e.activation(rstd, tl, AF.Exp, scale=-0.5),
             inc=sS, waits=[(sS, sS.n), (sV, ev["cmemset"])])
        sv_ = sS.n
        E.op("vector", lambda e: e.scalar_tensor_tensor(
            mrs, SM[:, lb + 48:lb + 49], -1.0, rstd, ALU.mult, ALU.mult),
            inc=sV, waits=[(sS, sv_)])
        vv2 = sV.n
        E.op("scalar", lambda e: e.activation(h1dst, src, AF.Identity,
                                              bias=mrs, scale=rstd),
             inc=sS, waits=[(sV, vv2),
                            (sT, ev.get((phase, "trdone", ts - 2)))])
        ev[(phase, "h1", ts)] = sS.n
        ev[(phase, "lnuse", ts)] = sS.n

    def p1_tr(ts):
        par = ts % 2
        h1tile = BIGW.ap()[:, par * 4096:par * 4096 + 4096]
        for g in range(4):
            gg = ts * 4 + g
            bank = PS[6 + gg % 2]
            bankbf = bank.ap().bitcast(BF16)
            w = [(sS, ev[("p1", "h1", ts)]),
                 (sV, ev.get(("p1trev", gg - 2)))]
            for k in range(8):
                cc = g * 8 + k
                E.op("tensor",
                     (lambda cc, k, bankbf, h1tile: lambda e: e.transpose(
                         bankbf[:, k * 128:k * 128 + 128],
                         h1tile[:, cc * 128:cc * 128 + 128], EYE))(cc, k, bankbf, h1tile),
                     inc=(sT if k == 7 else None), waits=(w if k == 0 else []))
            ev[("p1tr", gg)] = sT.n
            dst = h1T.rearrange("p (cc t) -> p cc t", t=1024)[
                :, g * 8:g * 8 + 8, ts * 128:ts * 128 + 128]
            E.op("vector",
                 (lambda dst, bankbf: lambda e: e.tensor_copy(
                     dst, bankbf[:, 0:1024].rearrange("p (a t) -> p a t", a=8)))(dst, bankbf),
                 inc=sV, waits=[(sT, ev[("p1tr", gg)])])
            ev[("p1trev", gg)] = sV.n
        ev[("p1", "trdone", ts)] = sT.n
        if ts == 3:
            ev[("h1T_half", 0)] = sV.n

    for ts in range(NTS):
        par = ts % 2
        xt = BIGVf[:, par * 4096:par * 4096 + 4096]
        E.op("scalar",
             (lambda ts, xt: lambda e: e.dma_start(
                 xt, x_d.ap()[ts * 128:ts * 128 + 128, :]))(ts, xt),
             inc=dXT[par], inc_n=16,
             waits=[(sS, ev.get(("p1", "h1", ts - 2)))])
        ev[("p1", "ld", ts)] = (dXT[par], dXT[par].n)
        h1tile = BIGW.ap()[:, par * 4096:par * 4096 + 4096]
        emit_ln(xt, ts, h1tile, "p1")
        if ts > 0:
            p1_tr(ts - 1)
    p1_tr(NTS - 1)
    ev[("h1T_half", 1)] = sV.n
    ev["h1T_done"] = sV.n


    E.op("sync",
         lambda e: e.dma_start(
             BDIAG[:, 0:2048].rearrange("p (h c) -> p h c", h=16),
             bdg_d.ap().rearrange("h (p c) -> p h c", p=128)),
         inc=dCst, inc_n=16, waits=[(dCst, ev["bdg_written"])])
    ev["bdiag_ready"] = dCst.n

    def _finish():
        allsems = [dCst, dRow, dRowB, dDbg] + dAo + dXT + dSg + dKs + dVs + dKT + dVh + dP6 + dQw + dOut
        E.wait("sync", [(s_, s_.n) for s_ in allsems if s_.n > 0])
        E.wait("gpsimd", [(CC, CC.n)] + [(s_, s_.n) for s_ in dQw + dOut if s_.n > 0])
        with nc.Block() as blk:
            blk.sync(lambda e: E.replay("sync", e))
            blk.gpsimd(lambda e: E.replay("gpsimd", e))
            blk.vector(lambda e: E.replay("vector", e))
            blk.scalar(lambda e: E.replay("scalar", e))
            blk.tensor(lambda e: E.replay("tensor", e))
        return nc

    if phases <= 1:
        return _finish()

    PREVW = {"wk": None, "wv": "wk", "wq": "wv"}

    def emit_strip_ld(wname, w_dram, ofG):
        buf = ofG % 2
        strip = BIGW.ap()[:, buf * 8192:buf * 8192 + 8192]
        prev = PREVW[wname]
        for hf in range(2):
            idx = ofG * 2 + hf
            sb = idx % 2
            stage = BIGVf[:, sb * 4096:sb * 4096 + 4096]
            wst = [(sV, ev.get((wname, "cv", idx - 2))),
                   (sS, ev.get((wname, "cvs", idx - 2)))]
            if idx < 2 and prev is not None:
                wst.append((sV, ev.get((prev, "cv", 30 + idx))))
                wst.append((sS, ev.get((prev, "cvs", 30 + idx))))
            if wname == "wk" and idx < 2:
                wst.append((sT, ev[("bdg_mm", 31)]))
                wst.append((sS, ev[("p1", "h1", NTS - 1)]))
            E.op("sync",
                 (lambda stage, idx: lambda e: e.dma_start(
                     stage,
                     w_dram.ap().rearrange("(i p) m -> i p m", p=128)[idx]))(stage, idx),
                 inc=dSg[sb], inc_n=16, waits=wst)
            ev[(wname, "stageld", idx)] = (dSg[sb], dSg[sb].n)

    def emit_strip_cv(wname, ofG):
        buf = ofG % 2
        strip = BIGW.ap()[:, buf * 8192:buf * 8192 + 8192]
        prev = PREVW[wname]
        for hf in range(2):
            idx = ofG * 2 + hf
            sb = idx % 2
            stage = BIGVf[:, sb * 4096:sb * 4096 + 4096]
            sdst = strip.rearrange("p (cc c) -> p cc c", c=256)[:, :, hf * 128:hf * 128 + 128]
            ssrc = stage.rearrange("p (cc c) -> p cc c", cc=32)
            cw = [ev[(wname, "stageld", idx)],
                  (sT, ev.get((wname, "stripuse", ofG - 2)))]
            if ofG < 2 and prev is not None:
                cw.append((sT, ev.get((prev, "stripuse", 14 + ofG))))
            if ofG < 2 and wname == "wk":
                cw.append((sT, ev.get(("p1", "trdone", NTS - 1))))
            if hf == 0:
                E.op("vector",
                     (lambda sdst, ssrc: lambda e: e.tensor_copy(sdst, ssrc))(sdst, ssrc),
                     inc=sV, waits=cw)
                ev[(wname, "cv", idx)] = sV.n
                ev[(wname, "cvs", idx)] = None
            else:
                E.op("scalar",
                     (lambda sdst, ssrc: lambda e: e.copy(sdst, ssrc))(sdst, ssrc),
                     inc=sS, waits=cw)
                ev[(wname, "cvs", idx)] = sS.n
                ev[(wname, "cv", idx)] = None
        ev[(wname, "conv_v", ofG)] = ev[(wname, "cv", ofG * 2)]
        ev[(wname, "conv_s", ofG)] = ev[(wname, "cvs", ofG * 2 + 1)]

    def emit_proj(wname, w_dram, bias_t, kind):
        emit_strip_ld(wname, w_dram, 0)
        emit_strip_cv(wname, 0)
        for ofG in range(16):
            if ofG + 1 < 16:
                emit_strip_ld(wname, w_dram, ofG + 1)
            buf = ofG % 2
            strip = BIGW.ap()[:, buf * 8192:buf * 8192 + 8192]
            for ofc2 in range(2):
                ofc = ofG * 2 + ofc2
                h = ofc // 2
                for th in range(2):
                    gi = (ofc * 2 + th) % 4
                    bank = PS[gi]
                    w0 = [(sV, ev[(wname, "conv_v", ofG)]),
                          (sS, ev[(wname, "conv_s", ofG)]),
                          (sV, ev[("h1T_half", th)]),
                          (sS, ev.get(("bankevac", gi))),
                          (dCst, ev["consts_loaded"])]
                    for cc in range(32):
                        lhs = strip[:, cc * 256 + ofc2 * 128:cc * 256 + ofc2 * 128 + 128]
                        rhs = h1T[:, cc * 1024 + th * 512:cc * 1024 + th * 512 + 512]
                        E.op("tensor",
                             (lambda bank, lhs, rhs, cc: lambda e: e.matmul(
                                 bank[:], lhs, rhs,
                                 start=(cc == 0), stop=(cc == 31)))(bank, lhs, rhs, cc),
                             inc=(sT if cc == 31 else None),
                             waits=(w0 if cc == 0 else []))
                    ev[(wname, "mm", ofc, th)] = sT.n
                    if ofc2 == 1 and th == 1:
                        ev[(wname, "stripuse", ofG)] = sT.n
                    if kind == "k" or kind == "v":
                        dst = SLOT.ap()[:, (ofc % 2) * 1024 + th * 512:
                                        (ofc % 2) * 1024 + th * 512 + 512]
                        wz = [(sT, ev[(wname, "mm", ofc, th)])]
                        if kind == "k":
                            kv_ = ev.get(("kstg_dma", ofc - 2))
                            if kv_ is not None:
                                wz.append(kv_)
                        else:
                            wz.append((sT, ev.get(("vtr", ofc - 2))))
                            if ofc < 2:
                                wz.append((dKs[0], ev["ck_done_0"]))
                                wz.append((dKs[1], ev["ck_done_1"]))
                    else:
                        dst = qT[:, ofc * 1024 + th * 512:ofc * 1024 + th * 512 + 512]
                        wz = [(sT, ev[(wname, "mm", ofc, th)]),
                              (dCst, ev["bdg_written"])]
                    E.op("scalar",
                         (lambda dst, bank, ofc: lambda e: e.activation(
                             dst, bank[:], AF.Identity,
                             bias=bias_t[:, ofc:ofc + 1]))(dst, bank, ofc),
                         inc=sS, waits=wz)
                    ev[("bankevac", gi)] = sS.n
                    ev[(wname, "evac", ofc, th)] = sS.n
                if kind == "k":
                    src = SLOT.ap()[:, (ofc % 2) * 1024:(ofc % 2) * 1024 + 1024]
                    E.op("sync",
                         (lambda src, ofc: lambda e: e.dma_start(
                             ck_d.ap()[ofc * 128:ofc * 128 + 128, :], src))(src, ofc),
                         inc=dKs[ofc % 2], inc_n=16,
                         waits=[(sS, ev[(wname, "evac", ofc, 1)])])
                    ev[("kstg_dma", ofc)] = (dKs[ofc % 2], dKs[ofc % 2].n)
                elif kind == "v":
                    bank = PS[6 + ofc % 2]
                    bankbf = bank.ap().bitcast(BF16)
                    wv = [(sS, ev[(wname, "evac", ofc, 1)]),
                          (sV, ev.get(("vtrev", ofc - 2)))]
                    for k in range(8):
                        src = SLOT.ap()[:, (ofc % 2) * 1024 + k * 128:
                                        (ofc % 2) * 1024 + k * 128 + 128]
                        E.op("tensor",
                             (lambda bankbf, src, k: lambda e: e.transpose(
                                 bankbf[:, k * 128:k * 128 + 128], src, EYE))(bankbf, src, k),
                             inc=(sT if k == 7 else None), waits=(wv if k == 0 else []))
                    ev[("vtr", ofc)] = sT.n
                    vstg = SLOT.ap()[:, 2048 + (ofc % 2) * 1024:
                                     2048 + (ofc % 2) * 1024 + 1024]
                    E.op("vector",
                         (lambda vstg, bankbf: lambda e: e.tensor_copy(
                             vstg, bankbf[:, 0:1024]))(vstg, bankbf),
                         inc=sV, waits=[(sT, ev[("vtr", ofc)])] +
                         ([ev[("vstg_dma", ofc - 2)]] if ("vstg_dma", ofc - 2) in ev else []))
                    ev[("vtrev", ofc)] = sV.n
                    E.op("sync",
                         (lambda vstg, ofc: lambda e: e.dma_start(
                             cv_d.ap()[:, ofc * 128:ofc * 128 + 128]
                             .rearrange("(a p) c -> p a c", p=128),
                             vstg.rearrange("p (a c) -> p a c", a=8)))(vstg, ofc),
                         inc=dVs[ofc % 2], inc_n=16, waits=[(sV, ev[("vtrev", ofc)])])
                    ev[("vstg_dma", ofc)] = (dVs[ofc % 2], dVs[ofc % 2].n)
                else:
                    for th in range(2):
                        bank = PS[4 + th]
                        src = qT[:, ofc * 1024 + th * 512:ofc * 1024 + th * 512 + 512]
                        E.op("tensor",
                             (lambda bank, h, src: lambda e: e.matmul(
                                 bank[:], BDIAG[:, h * 128:h * 128 + 128], src,
                                 start=True, stop=True))(bank, h, src),
                             inc=sT,
                             waits=[(sS, ev[(wname, "evac", ofc, th)]),
                                    (dCst, ev["bdiag_ready"]),
                                    (sS, ev.get(("bdq_ev", ofc - 1, th)))])
                        ev[("bdq_mm", ofc, th)] = sT.n
                        E.op("scalar",
                             (lambda src, bank: lambda e: e.activation(
                                 src, bank[:], AF.Identity))(src, bank),
                             inc=sS, waits=[(sT, ev[("bdq_mm", ofc, th)])])
                        ev[("bdq_ev", ofc, th)] = sS.n
            if ofG + 1 < 16:
                emit_strip_cv(wname, ofG + 1)

    emit_proj("wk", wkT_d, BK, "k")
    ev["ck_done_0"] = dKs[0].n
    ev["ck_done_1"] = dKs[1].n
    E.op("gpsimd",
         lambda e: e.collective_compute(
             "AllGather", ALU.bypass,
             replica_groups=[[0, 1, 2, 3], [4, 5, 6, 7]],
             ins=[ck_d.ap().opt()], outs=[gk_d.ap().opt()]),
         inc=CC, waits=[(dKs[0], ev["ck_done_0"]), (dKs[1], ev["ck_done_1"])])
    if phases <= 2:
        return _finish()
    emit_proj("wv", wvT_d, BV, "v")
    ev["cv_done_0"] = dVs[0].n
    ev["cv_done_1"] = dVs[1].n
    E.op("gpsimd",
         lambda e: e.collective_compute(
             "AllGather", ALU.bypass,
             replica_groups=[[0, 1, 2, 3], [4, 5, 6, 7]],
             ins=[cv_d.ap().opt()], outs=[gv_d.ap().opt()]),
         inc=CC, waits=[(dVs[0], ev["cv_done_0"]), (dVs[1], ev["cv_done_1"]), (CC, 1)])
    if phases <= 3:
        return _finish()
    emit_proj("wq", wqT_d, BQ, "q")
    ev["qT_done"] = sS.n
    for b in range(2):
        E.op("vector",
             (lambda b: lambda e: e.memset(
                 BIGV.ap()[:, b * 8224 + 256:b * 8224 + 8224:257], 1.0))(b),
             inc=sV, waits=[(sV, ev[("wq", "conv_v", 15)]),
                            (sS, ev[("wq", "conv_s", 15)])])
    ev["vhat_ones"] = sV.n

    if phases <= 4:
        return _finish()
    def emit_head_loads(h):
        hb = h % 2
        ktbuf = BIGW.ap()[:, hb * 8192:hb * 8192 + 8192]
        wk_ = [(CC, 1), (sT, ev.get(("kt_use", h - 2)))]
        if h < 2:
            wk_.append((sT, ev[("wq", "stripuse", 14 + h)]))
        for dc in range(2):
            ktsrc = gk_d.ap().rearrange("(r hh dc p) t -> hh dc p r t",
                                        r=4, hh=16, dc=2, p=128)[h, dc]
            ktdst = ktbuf.rearrange("p (dc r t) -> p dc r t", dc=2, r=4)[:, dc]
            E.op("sync",
                 (lambda ktdst, ktsrc: lambda e: e.dma_start(ktdst, ktsrc))(ktdst, ktsrc),
                 inc=dKT[hb], inc_n=16, waits=(wk_ if dc == 0 else []))
        ev[("kt_ld", h)] = (dKT[hb], dKT[hb].n)
        vb = BIGV.ap()[:, hb * 8224:hb * 8224 + 8224]
        vdst = vb.rearrange("p (mc j) -> p mc j", j=257)[:, :, 0:256]
        vsrc = gv_d.ap().rearrange("(a p) c -> p a c", p=128)[:, :, h * 256:h * 256 + 256]
        wv_ = [(CC, 2), (sV, ev["vhat_ones"]),
               (sT, ev.get(("v_use", h - 2)))]
        if h < 2:
            wv_ += [(sV, ev[("wq", "conv_v", 15)]), (sS, ev[("wq", "conv_s", 15)])]
        E.op("sync",
             (lambda vdst, vsrc: lambda e: e.dma_start(vdst, vsrc))(vdst, vsrc),
             inc=dVh[hb], inc_n=16, waits=wv_)
        ev[("v_ld", h)] = (dVh[hb], dVh[hb].n)

    emit_head_loads(0)
    for h in range(NH):
        hb = h % 2
        ktbuf = BIGW.ap()[:, hb * 8192:hb * 8192 + 8192]
        vb = BIGV.ap()[:, hb * 8224:hb * 8224 + 8224]
        if h + 1 < NH:
            emit_head_loads(h + 1)

        for tg in range(2):
            for mp in range(17):
                for mc in (2 * mp, 2 * mp + 1):
                    if mc >= 32:
                        continue
                    lb = PS[4 + mc % 4]
                    wl = []
                    if mc % 2 == 0:
                        wl = [ev[("kt_ld", h)],
                              (sS, ev.get(("exp", h, tg, mc - 3),
                                          ev.get(("exp", h, tg - 1, 29),
                                                 ev.get(("exp", h - 1, 1, 29))))),
                              (sS, ev["qT_done"])]
                    for dc in range(2):
                        lhs = ktbuf[:, dc * 4096 + mc * 128:dc * 4096 + mc * 128 + 128]
                        rhs = qT[:, (2 * h + dc) * 1024 + tg * 512:
                                 (2 * h + dc) * 1024 + tg * 512 + 512]
                        E.op("tensor",
                             (lambda lb, lhs, rhs, dc: lambda e: e.matmul(
                                 lb[:], lhs, rhs,
                                 start=(dc == 0), stop=(dc == 1)))(lb, lhs, rhs, dc),
                             inc=(sT if dc == 1 else None),
                             waits=(wl if dc == 0 else []))
                    ev[("log", h, tg, mc)] = sT.n
                    pt = SLOT.ap()[:, (mc % 4) * 512:(mc % 4) * 512 + 512]
                    we = [(sT, ev[("log", h, tg, mc)]),
                          (sT, ev.get(("pv", h, tg, mc - 4),
                                      ev.get(("pv", h, tg - 1, 31),
                                             ev.get(("pv", h - 1, 1, 31))))),
                          (sV, ev["cmemset"])]
                    if h == 0 and tg == 0 and mc < 2:
                        we.append(ev[("vstg_dma", 31)])
                    E.op("scalar",
                         (lambda pt, lb: lambda e: e.activation(
                             pt, lb[:], AF.Exp, bias=-SOFTMAX_C))(pt, lb),
                         inc=sS, waits=we)
                    ev[("exp", h, tg, mc)] = sS.n
                for pv in (2 * mp - 2, 2 * mp - 1):
                    if not (0 <= pv < 32):
                        continue
                    pt = SLOT.ap()[:, (pv % 4) * 512:(pv % 4) * 512 + 512]
                    wp = []
                    if pv % 2 == 0:
                        wp = [(sS, ev[("exp", h, tg, pv + 1)]),
                              ev[("v_ld", h)]]
                        if pv == 0:
                            nrm = ev.get(("norm", h, tg - 1), ev.get(("norm", h - 1, 1)))
                            if nrm is not None:
                                wp.append((sS, nrm[1]))
                    for t4 in range(4):
                        bank = PS[t4]
                        lhs = pt[:, t4 * 128:t4 * 128 + 128]
                        rhs = vb[:, pv * 257:pv * 257 + 257]
                        E.op("tensor",
                             (lambda bank, lhs, rhs, pv: lambda e: e.matmul(
                                 bank[:, 0:257], lhs, rhs,
                                 start=(pv == 0), stop=(pv == 31)))(bank, lhs, rhs, pv),
                             inc=(sT if t4 == 3 else None),
                             waits=(wp if t4 == 0 else []))
                    ev[("pv", h, tg, pv)] = sT.n
            ev[("kt_use", h)] = sT.n
            ev[("v_use", h)] = sT.n
            for t4 in range(4):
                bank = PS[t4]
                rs = SM[:, 58 + t4:59 + t4]
                E.op("vector",
                     (lambda bank, rs: lambda e: e.reciprocal(
                         rs, bank[:, 256:257]))(bank, rs),
                     inc=(sV if t4 == 3 else None),
                     waits=([(sT, ev[("pv", h, tg, 31)])] +
                            ([ev[("ao_dma", h - 2)]] if ("ao_dma", h - 2) in ev else [])
                            if t4 == 0 else []))
            vv = sV.n
            for t4 in range(4):
                bank = PS[t4]
                rs = SM[:, 58 + t4:59 + t4]
                dst = BIGA.ap()[:, (h % 2) * 2048 + (tg * 4 + t4) * 256:
                                (h % 2) * 2048 + (tg * 4 + t4) * 256 + 256]
                E.op("scalar",
                     (lambda dst, bank, rs: lambda e: e.activation(
                         dst, bank[:, 0:256], AF.Identity, scale=rs))(dst, bank, rs),
                     inc=sS, waits=[(sV, vv)])
            ev[("norm", h, tg)] = ("S", sS.n)
        E.op("sync",
             (lambda h: lambda e: e.dma_start(
                 ao_d.ap()[:, h * 256:h * 256 + 256]
                 .rearrange("(a p) c -> p a c", p=128),
                 BIGA.ap()[:, (h % 2) * 2048:(h % 2) * 2048 + 2048]
                 .rearrange("p (a c) -> p a c", a=8)))(h),
             inc=dAo[h % 2], inc_n=16, waits=[(sS, ev[("norm", h, 1)][1])])
        ev[("ao_dma", h)] = (dAo[h % 2], dAo[h % 2].n)
    ev["attn_done"] = [(dAo[0], dAo[0].n), (dAo[1], dAo[1].n)]

    if debug:
        E.op("sync", lambda e: e.dma_start(dbg_a.ap(), BIGA.ap()[:]),
             inc=dDbg, inc_n=16, waits=[(sS, ev["qT_done"])])
        E.op("sync", lambda e: e.dma_start(dbg_b.ap(), BIGB.ap()[:]),
             inc=dDbg, inc_n=16, waits=[(sT, ev[("kt_use", NH - 1)])])
        E.op("sync", lambda e: e.dma_start(dbg_gk.ap(), gk_d.ap()),
             inc=dDbg, inc_n=16, waits=[(CC, 2)])
        E.op("sync", lambda e: e.dma_start(dbg_gv.ap(), gv_d.ap()),
             inc=dDbg, inc_n=16)
        E.op("sync", lambda e: e.dma_start(dbg_ao.ap(), ao_d.ap()),
             inc=dDbg, inc_n=16, waits=list(ev["attn_done"]))
        E.op("sync", lambda e: e.dma_start(dbg_bd.ap(), bdg_d.ap()),
             inc=dDbg, inc_n=16, waits=[(dCst, ev["bdg_written"])])
        ev["dbg_done"] = dDbg.n

    if phases <= 5:
        return _finish()
    ev["_dq_early"] = True
    E.op("sync", lambda e: e.dma_start(EYE, eye_d.ap()),
         inc=dCst, inc_n=16,
         waits=[(sT, ev[("pv", NH - 1, 1, 31)]),
                (sS, ev[("exp", NH - 1, 1, 31)])])
    ev["eye2"] = dCst.n
    def p6_tr(ts):
        par = ts % 2
        h2tile = BIGW.ap()[:, 8192 + par * 4096:8192 + par * 4096 + 4096]
        for g in range(4):
            gg = ts * 4 + g
            bank = PS[6 + gg % 2]
            bankbf = bank.ap().bitcast(BF16)
            pr = g // 2
            w = [(sS, ev[("p6", "h1", ts)]),
                 (sV, ev.get(("p6trev", gg - 2))),
                 (dCst, ev["eye2"])]
            for k in range(8):
                jc = (g % 2) * 8 + k
                src_ = h2tile[:, jc * 256 + pr:jc * 256 + pr + 255:2]
                E.op("tensor",
                     (lambda bankbf, src_, k: lambda e: e.transpose(
                         bankbf[:, k * 128:k * 128 + 128], src_, EYE))(bankbf, src_, k),
                     inc=(sT if k == 7 else None), waits=(w if k == 0 else []))
            ev[("p6tr", gg)] = sT.n
            st = pr * 16 + (g % 2) * 8
            dst = BIGA.ap().rearrange("p (a t) -> p a t", t=1024)[
                :, st:st + 8, ts * 128:ts * 128 + 128]
            E.op("vector",
                 (lambda dst, bankbf: lambda e: e.tensor_copy(
                     dst, bankbf[:, 0:1024].rearrange("p (a t) -> p a t", a=8)))(dst, bankbf),
                 inc=sV, waits=[(sT, ev[("p6tr", gg)])])
            ev[("p6trev", gg)] = sV.n
        ev[("p6", "trdone", ts)] = sT.n

    for ts in range(NTS):
        par = ts % 2
        xt = BIGVf[:, par * 4096:par * 4096 + 4096]
        aot = BIGW.ap()[:, par * 4096:par * 4096 + 4096]
        _x2p = ev.get(("p6", "x2", ts - 2))
        wx6 = list(ev["attn_done"]) + [
               (CC, _x2p[1]) if _x2p else None,
               (sT, ev[("v_use", NH - 1)])]
        wx6 = [w_ for w_ in wx6 if w_ is not None]
        if debug:
            wx6.append((dDbg, ev["dbg_done"]))
        E.op("sync",
             (lambda xt, ts: lambda e: e.dma_start(
                 xt, x_d.ap()[ts * 128:ts * 128 + 128, :]))(xt, ts),
             inc=dP6[par], inc_n=16, waits=wx6)
        E.op("sync",
             (lambda aot, ts: lambda e: e.dma_start(
                 aot, ao_d.ap()[ts * 128:ts * 128 + 128, :]))(aot, ts),
             inc=dP6[par], inc_n=16,
             waits=[(sT, ev[("kt_use", NH - 1)])])
        ev[("p6", "ld", ts)] = (dP6[par], dP6[par].n)
        x2sl = x2res[:, ts * 4096:ts * 4096 + 4096]
        E.op("gpsimd",
             (lambda x2sl, xt, aot: lambda e: e.tensor_tensor(
                 x2sl, xt, aot, ALU.add))(x2sl, xt, aot),
             inc=CC, waits=[ev[("p6", "ld", ts)],
                            (CC, 2 + ts)])
        ev[("p6", "x2", ts)] = ("CCg", CC.n)
        h2tile = BIGW.ap()[:, 8192 + par * 4096:8192 + par * 4096 + 4096]
        emit_ln(x2sl, ts, h2tile, "p6")
        if ts > 0:
            p6_tr(ts - 1)
    p6_tr(NTS - 1)
    ev["h2T_done"] = sV.n

    if phases <= 6:
        return _finish()
    wrow = [(sT, ev[("v_use", NH - 1)]), (sV, ev["h2T_done"]),
            (CC, ev[("p6", "x2", NTS - 1)][1])]
    E.op("vector", lambda e: e.memset(ONES_ROW, 1.0), inc=sV, waits=wrow)
    ev["ones_row"] = sV.n
    ROWBUF = [(SMALL2.ap()[0:1, 0:512], SMALL2.ap()[0:1, 512:1024]),
              (BIGVf[0:1, 6656:7168], SLOT.ap().bitcast(F32)[0:1, 0:512])]
    dRow2 = [dRow, dRowB]
    for i in range(8):
        sch0, zch0 = ROWBUF[i % 2]
        dr = dRow2[i % 2]
        E.op("sync",
             (lambda sch0, i: lambda e: e.dma_start(
                 sch0, fs_d.ap()[:, i * 512:i * 512 + 512]))(sch0, i),
             inc=dr, inc_n=16,
             waits=[(sV, ev.get(("rowc", i - 2), ev["ones_row"]))])
        E.op("sync",
             (lambda zch0, i: lambda e: e.dma_start(
                 zch0, fz_d.ap()[:, i * 512:i * 512 + 512]))(zch0, i),
             inc=dr, inc_n=16)
        lvz = (dr, dr.n)
        E.op("vector",
             (lambda sch0, i: lambda e: e.tensor_copy(
                 SROWBF[:, i * 512:i * 512 + 512], sch0))(sch0, i),
             inc=sV, waits=[lvz])
        E.op("vector",
             (lambda zch0, i: lambda e: e.tensor_scalar_mul(
                 ZNEG[:, i * 512:i * 512 + 512], zch0, -1.0))(zch0, i),
             inc=sV, waits=[(sV, sV.n)])
        ev[("rowc", i)] = sV.n
    ev["rows_ld"] = dRow.n
    ev["zneg"] = sV.n
    if phases == 61:
        return _finish()
    BOSBUF = [(SMALL2.ap()[0:1, 0:512], SMALL2.ap()[0:1, 512:1024],
               SLOT.ap().bitcast(F32)[0:1, 512:1024]),
              (BIGVf[0:1, 6656:7168], SLOT.ap().bitcast(F32)[0:1, 0:512],
               SLOT.ap().bitcast(F32)[0:1, 1536:2048])]
    for i in range(8):
        sch, rch, bch = BOSBUF[i % 2]
        dr = dRow2[i % 2]
        E.op("sync",
             (lambda sch, i: lambda e: e.dma_start(
                 sch, fs_d.ap()[:, i * 512:i * 512 + 512]))(sch, i),
             inc=dr, inc_n=16,
             waits=[(sV, ev.get(("bos", i - 2), ev["zneg"]))])
        E.op("sync",
             (lambda bch, i: lambda e: e.dma_start(
                 bch, fb_d.ap()[:, i * 512:i * 512 + 512]))(bch, i),
             inc=dr, inc_n=16)
        lv = (dr, dr.n)
        E.op("vector",
             (lambda rch, sch: lambda e: e.reciprocal(rch, sch))(rch, sch),
             inc=sV, waits=[lv])
        vv = sV.n
        E.op("vector",
             (lambda rch, bch, i: lambda e: e.tensor_tensor(
                 BOS[:, i * 512:i * 512 + 512], bch, rch, ALU.mult))(rch, bch, i),
             inc=sV, waits=[(sV, vv)])
        ev[("bos", i)] = sV.n
    ev["bos_done"] = sV.n
    if phases == 62:
        return _finish()
    sbc = BIGWf[:, 0:4096]
    for i in range(8):
        bank = PS[4 + i % 2]
        E.op("tensor",
             (lambda i, bank: lambda e: e.matmul(
                 bank[:], ONES_ROW, SROWBF[:, i * 512:i * 512 + 512],
                 start=True, stop=True))(i, bank),
             inc=sT, waits=[(sV, ev["zneg"]),
                            ((sS, ev[("sbc", i - 2)][1]) if ("sbc", i - 2) in ev
                             else (sV, ev["ones_row"])),
                            (sS, ev[("exp", NH - 1, 1, 31)])])
        tv_ = sT.n
        E.op("scalar",
             (lambda i, bank: lambda e: e.copy(
                 sbc[:, i * 512:i * 512 + 512], bank[:]))(i, bank),
             inc=sS, waits=[(sT, tv_), (CC, ev[("p6", "x2", NTS - 1)][1])])
        ev[("sbc", i)] = ("S", sS.n)
    ev["sbc_done"] = ("S", sS.n)
    if phases == 63:
        return _finish()
    for th in range(2):
        bank = PS[6 + th]
        for idx in range(32):
            jc, pr = idx % 16, idx // 16
            rhs = BIGA.ap()[:, pr * 16384 + jc * 1024 + th * 512:
                            pr * 16384 + jc * 1024 + th * 512 + 512]
            E.op("tensor",
                 (lambda rhs, idx, bank: lambda e: e.matmul(
                     bank[0:1, :], ONES128, rhs,
                     start=(idx == 0), stop=(idx == 31)))(rhs, idx, bank),
                 inc=(sT if idx == 31 else None),
                 waits=([(sV, ev["h2T_done"]),
                         (sV, ev.get(("p6trev", NTS * 4 - 1 - th)))] if idx == 0 else []))
        ev[("srow_mm", th)] = sT.n
        E.op("vector",
             (lambda th, bank: lambda e: e.tensor_copy(
                 S_ROW[:, th * 512:th * 512 + 512], bank[0:1, :]))(th, bank),
             inc=sV, waits=[(sT, ev[("srow_mm", th)])])
        ev[("srow_ev", th)] = sV.n
    ev["srow_done"] = sV.n

    if phases <= 7:
        return _finish()
    if debug:
        E.op("sync", lambda e: e.dma_start(dbg_h2.ap(), BIGA.ap()[:]),
             inc=dDbg, inc_n=16, waits=[(sV, ev["h2T_done"])])
        E.op("sync", lambda e: e.dma_start(dbg_x2.ap(), BIGB.ap()[:]),
             inc=dDbg, inc_n=16)
        E.op("sync", lambda e: e.dma_start(dbg_rw.ap(), BIGV.ap().bitcast(I32)[0:1, 0:7232]),
             inc=dDbg, inc_n=16,
             waits=[(sV, ev["srow_done"]), (sV, ev["bos_done"]), (sV, ev["zneg"])])
        E.op("sync", lambda e: e.dma_start(dbg_sb.ap(), BIGWf[:, 0:4096]),
             inc=dDbg, inc_n=16, waits=[(sS, ev["sbc_done"][1])])
        ev["dbg2_done"] = dDbg.n

    dq_done = set()

    def emit_dq(ofb, jc):
        if (ofb, jc) in dq_done or ofb >= 8:
            return
        dq_done.add((ofb, jc))
        par = jc % 2
        qb = BIGWi[:, 4096 + par * 512:4096 + par * 512 + 512]
        ih = BIGWi[:, 5120 + par * 1024:5120 + par * 1024 + 512]
        il = BIGWi[:, 5120 + par * 1024 + 512:5120 + par * 1024 + 1024]
        nh_ = BIGW.ap()[:, 14336 + par * 1024:14336 + par * 1024 + 512]
        nl_ = BIGW.ap()[:, 14336 + par * 1024 + 512:14336 + par * 1024 + 1024]
        gidx = ofb * 16 + jc
        E.op("gpsimd",
             (lambda qb, ofb, jc: lambda e: e.dma_start(
                 qb, qwT_d.ap()[jc * 128:jc * 128 + 128,
                                ofb * 512:ofb * 512 + 512]))(qb, ofb, jc),
             inc=dQw[par], inc_n=16,
             waits=[((sV, ev[("unp", gidx - 2)]) if ("unp", gidx - 2) in ev
                     else None),
                    (sT, ev.get(("p6", "trdone", NTS - 1))),
                    (sT, ev[("kt_use", NH - 1)])])
        lv = (dQw[par], dQw[par].n)
        E.op("vector",
             (lambda ih, qb: lambda e: e.tensor_scalar(
                 ih, qb, 4, 15, ALU.logical_shift_right, ALU.bitwise_and))(ih, qb),
             inc=sV, waits=[lv, (sS, ev.get(("cvt", gidx - 2)))])
        E.op("vector",
             (lambda il, qb: lambda e: e.tensor_scalar(
                 il, qb, 15, None, ALU.bitwise_and))(il, qb),
             inc=sV, waits=[(sV, sV.n)])
        ev[("unp", gidx)] = sV.n
        wc = [(sV, ev[("unp", gidx)]),
              (sT, ev.get(("nibuse", gidx - 2)))]
        E.op("scalar",
             (lambda nh_, ih: lambda e: e.activation(nh_, ih, AF.Copy))(nh_, ih),
             inc=sS, waits=wc)
        E.op("scalar",
             (lambda nl_, il: lambda e: e.activation(nl_, il, AF.Copy))(nl_, il),
             inc=sS, waits=[(sS, sS.n)])
        ev[("cvt", gidx)] = sS.n

    emit_dq(0, 0)
    emit_dq(0, 1)
    for ofb in range(8):
        for jc in range(16):
            emit_dq(ofb, jc)
            par = jc % 2
            gidx = ofb * 16 + jc
            nh_ = BIGW.ap()[:, 14336 + par * 1024:14336 + par * 1024 + 512]
            nl_ = BIGW.ap()[:, 14336 + par * 1024 + 512:14336 + par * 1024 + 1024]
            for pr, nib in ((0, nh_), (1, nl_)):
                for ts in range(NTS):
                    bank = PS[ts]
                    lhs = BIGA.ap()[:, pr * 16384 + jc * 1024 + ts * 128:
                                    pr * 16384 + jc * 1024 + ts * 128 + 128]
                    w0 = []
                    if ts == 0:
                        w0.append((sS, ev[("cvt", gidx)]))
                    if jc == 0 and pr == 0:
                        w0.append((sV, ev.get(("epi", ofb - 1, ts))))
                        if ofb == 0:
                            w0 += [(sV, ev["srow_done"]), (sV, ev["bos_done"]),
                                   (sV, ev["zneg"])]
                    E.op("tensor",
                         (lambda bank, lhs, nib, jc, pr: lambda e: e.matmul(
                             bank[:], lhs, nib,
                             start=(jc == 0 and pr == 0), stop=False))(bank, lhs, nib, jc, pr),
                         inc=(sT if (pr == 1 and ts == NTS - 1) else None),
                         waits=w0)
            ev[("nibuse", gidx)] = sT.n
        emit_dq(ofb + 1, 0)
        emit_dq(ofb + 1, 1)
        for ts in range(NTS):
            bank = PS[ts]
            if phases != 81:
                E.op("tensor",
                     (lambda bank, ts, ofb: lambda e: e.matmul(
                         bank[:], S_ROW[:, ts * 128:ts * 128 + 128],
                         ZNEG[:, ofb * 512:ofb * 512 + 512],
                         start=False, stop=False))(bank, ts, ofb))
                E.op("tensor",
                     (lambda bank, ofb: lambda e: e.matmul(
                         bank[:], ONES_ROW,
                         BOS[:, ofb * 512:ofb * 512 + 512],
                         start=False, stop=True))(bank, ofb),
                     inc=sT)
            else:
                E.op("tensor",
                     (lambda bank, ofb: lambda e: e.matmul(
                         bank[:], ONES_ROW,
                         BOS[:, ofb * 512:ofb * 512 + 512],
                         start=False, stop=True))(bank, ofb),
                     inc=sT)
            ev[("ffn_mm", ofb, ts)] = sT.n
            ob = ts % 2
            t1 = SLOT.ap().bitcast(F32)[:, 1024 + ob * 512:1024 + ob * 512 + 512]
            o1 = SMALL2.ap()[:, ob * 512:ob * 512 + 512]
            E.op("vector",
                 (lambda t1, bank, ofb: lambda e: e.tensor_tensor(
                     t1, bank[:], sbc[:, ofb * 512:ofb * 512 + 512],
                     ALU.mult))(t1, bank, ofb),
                 inc=sV, waits=[(sT, ev[("ffn_mm", ofb, ts)]),
                                (sV, ev.get(("epi", ofb, ts - 2),
                                            ev.get(("epi", ofb - 1, NTS - 2 + ob))))])
            ev[("epi_t1", ofb, ts)] = sV.n
            E.op("vector",
                 (lambda o1, t1, ofb, ts: lambda e: e.tensor_tensor(
                     o1, t1, x2res[:, ts * 4096 + ofb * 512:ts * 4096 + ofb * 512 + 512],
                     ALU.add))(o1, t1, ofb, ts),
                 inc=sV, waits=[(sV, ev[("epi_t1", ofb, ts)])] +
                 ([ev[("out_dma", ofb, ts - 2)]] if ("out_dma", ofb, ts - 2) in ev
                  else ([ev[("out_dma", ofb - 1, ts + 6)]]
                        if ("out_dma", ofb - 1, ts + 6) in ev else [])))
            ev[("epi", ofb, ts)] = sV.n
            E.op("gpsimd",
                 (lambda o1, ofb, ts: lambda e: e.dma_start(
                     out_d.ap()[ts * 128:ts * 128 + 128,
                                ofb * 512:ofb * 512 + 512], o1))(o1, ofb, ts),
                 inc=dOut[ob], inc_n=16, waits=[(sV, ev[("epi", ofb, ts)])])
            ev[("out_dma", ofb, ts)] = (dOut[ob], dOut[ob].n)

    if phases <= 7:
        return _finish()
    return _finish()


_NC_CACHE = None


def kernel(L0_attn_x, L0_attn_Wq, L0_attn_bq, L0_attn_Wk, L0_attn_bk,
           L0_attn_Wv, L0_attn_bv, L0_attn_bias_weight, L0_ffn_qweight,
           L0_ffn_scale, L0_ffn_zero, L0_ffn_bias):
    global _NC_CACHE
    f32 = np.float32
    x = np.ascontiguousarray(np.asarray(L0_attn_x, f32).reshape(2 * 4096, DM))
    def _halfstrip(w):
        wT = np.asarray(w, f32).T
        return np.ascontiguousarray(
            wT.reshape(32, 128, 32, 128).transpose(2, 1, 0, 3).reshape(4096, 4096))
    wqT = _halfstrip(L0_attn_Wq)
    wkT = _halfstrip(L0_attn_Wk)
    wvT = _halfstrip(L0_attn_Wv)
    qwT = np.ascontiguousarray(np.asarray(L0_ffn_qweight, np.int32).T)
    bq_t = np.ascontiguousarray(np.asarray(L0_attn_bq, f32).reshape(32, 128).T)
    bk_t = np.ascontiguousarray(np.asarray(L0_attn_bk, f32).reshape(32, 128).T)
    bv_t = np.ascontiguousarray(np.asarray(L0_attn_bv, f32).reshape(32, 128).T)
    bwT = np.ascontiguousarray(np.asarray(L0_attn_bias_weight, f32).T)
    fs = np.ascontiguousarray(np.asarray(L0_ffn_scale, f32).reshape(1, DM))
    fz = np.ascontiguousarray(np.asarray(L0_ffn_zero, f32).reshape(1, DM))
    fb = np.ascontiguousarray(np.asarray(L0_ffn_bias, f32).reshape(1, DM))
    bdc4, eye = _host_consts()

    if _NC_CACHE is None:
        _NC_CACHE = build_nc()
    nc = _NC_CACHE
    in_maps = []
    for c in range(NCORES):
        in_maps.append({
            "x": np.ascontiguousarray(x[c * R:(c + 1) * R]),
            "wqT": wqT, "wkT": wkT, "wvT": wvT, "qwT": qwT,
            "bq_t": bq_t, "bk_t": bk_t, "bv_t": bv_t, "bwT": bwT,
            "fscale": fs, "fzero": fz, "fbias": fb,
            "eye": eye, "bdc": bdc4,
        })
    res = run_bass_kernel_spmd(nc, in_maps, core_ids=list(range(NCORES)))
    out = np.concatenate([np.asarray(r["out"]) for r in res.results], axis=0)
    return out.reshape(2, 4096, DM)
